# version 03
# baseline (speedup 1.0000x reference)
import numpy as np
import ml_dtypes
import concourse.bass as bass
import concourse.mybir as mybir
from concourse.bass_utils import run_bass_kernel_spmd

F32 = mybir.dt.float32
BF16 = mybir.dt.bfloat16
AF = mybir.ActivationFunctionType
ALU = mybir.AluOpType
AX = mybir.AxisListType
f32 = np.float32

T = 2048
HT = 1024
D = 1024
NCH = 8
NH = 16
DH = 64
DHID = 4096
NEG = -1.0e30

A_XT    = 0
A_QKVT  = 65536
A_G     = 65536
A_XREL  = 131072
A_XHAT  = 163840
A_S0    = 163840
A_P0    = 172032
A_P1    = 176128
A_WLD0  = 180224
A_WLD1  = 184320
A_WBF0  = 188416
A_WBF1  = 196608
A_XBF   = 180224
A_SQBF  = 182272
A_MEANB = 184320
A_RSTHI = 186368
A_RSTLO = 188416
A_SCRM  = 190464
A_SCRV  = 194560
A_XHTMP = 198656
A_PTT0  = 180224
A_PTT1  = 184320
A_IDENT = 204800
A_MASK  = 205056
A_ONES  = 205568
A_SVEC  = 205824
A_NMAX  = 206528
A_CZERO = 206592
A_SILU  = 206656
A_FFTMP = 208704
A_RECR  = 210752
A_RECH  = 211776
A_RECL  = 212288
A_VP0   = 188416
A_VP1   = 190528
A_RREP  = 192640
ARENA_BYTES = 212800

SV_L0SC = 0
SV_L0BI = 8
SV_BOUT0 = 16
SV_BOUT1 = 24
SV_W1S = 32
SV_W2S = 64
SV_H1S = 96
SV_H2S = 128
SV_W3S = 160
SV_L0SC2 = 168
SV_COLS = 176


def _blk(w):
    O, I = w.shape
    G, IC = O // 128, I // 128
    wt = np.ascontiguousarray(np.asarray(w).T)
    return np.ascontiguousarray(
        wt.reshape(IC, 128, G, 128).transpose(2, 1, 0, 3))


def _svpack(v):
    v = np.asarray(v, dtype=np.float32)
    return np.ascontiguousarray(v.reshape(-1, 128).T)


def build_nc(stop_after="full"):
    nc = bass.Bass("TRN2", target_bir_lowering=False)

    dr = {}
    def din(name, shape, dt=F32):
        dr[name] = nc.dram_tensor(name, shape, dt, kind="ExternalInput").ap()
        return dr[name]

    din("xT", [D, T])
    din("wq0", [8, 128, 8, 128]); din("wk0", [8, 128, 8, 128])
    din("wv0", [8, 128, 8, 128]); din("wo0", [8, 128, 8, 128])
    din("wf0", [8, 128, 8, 128])
    din("wqkv1", [24, 128, 8, 128]); din("wo1", [8, 128, 8, 128])
    din("w1", [32, 128, 8, 128]); din("w2", [32, 128, 8, 128])
    din("h1", [32, 128, 32, 128]); din("h2", [32, 128, 32, 128])
    din("w3", [8, 128, 32, 128])
    din("svec", [128, 168])
    din("ident", [128, 128], BF16)
    din("maskc", [128, 128])
    outT = nc.dram_tensor("outT", [D, T], F32, kind="ExternalOutput").ap()
    xsp = nc.dram_tensor("xspill", [D, T], F32, kind="Internal").ap()

    from contextlib import ExitStack
    ctx = ExitStack()
    E = ctx.enter_context

    backing = E(nc.sbuf_tensor("arena", [128, ARENA_BYTES], mybir.dt.uint8))
    base = nc.lookup_mloc(backing).addr

    def sb(name, off, shape, dtype):
        h = nc.virtual_sbuf_tensor(name, shape, dtype)
        nc._tensor(h.name, shape, dtype, type="SB")
        ml = nc.lookup_mloc(h)
        ml.allocated = True
        ml.addr = base + off
        return h

    xT   = sb("xT_sb",  A_XT,   [128, 16384], F32)
    h_sb = sb("h_sb",   A_XT,   [128, 32768], BF16)
    qkvT = sb("qkvT",   A_QKVT, [128, 49152], BF16)
    g_sb = sb("g_sb",   A_G,    [128, 32768], BF16)
    xrel = sb("xrel",   A_XREL, [128, 8192],  F32)
    xhat = sb("xhat",   A_XHAT, [128, 8192],  BF16)
    S_sb = [sb("S0", A_S0, [128, 2048], F32), sb("S0b", A_S0, [128, 2048], F32)]
    P_sb = [sb("P0", A_P0, [128, 2048], BF16), sb("P1", A_P1, [128, 2048], BF16)]
    wld  = [sb("wld0", A_WLD0, [128, 1024], F32), sb("wld1", A_WLD1, [128, 1024], F32)]
    wbf  = [sb("wbf0", A_WBF0, [128, 4096], BF16), sb("wbf1", A_WBF1, [128, 4096], BF16)]
    xbf  = sb("xbf",   A_XBF,   [128, 1024], BF16)
    sqbf = sb("sqbf",  A_SQBF,  [128, 1024], BF16)
    meanb = sb("meanb", A_MEANB, [1, 1024], BF16)
    rsthi = sb("rsthi", A_RSTHI, [1, 1024], BF16)
    rstlo = sb("rstlo", A_RSTLO, [1, 1024], BF16)
    scrm = sb("scrm",  A_SCRM,  [1, 1024], F32)
    scrv = sb("scrv",  A_SCRV,  [1, 1024], F32)
    xhtmp = sb("xhtmp", A_XHTMP, [128, 512], F32)
    ptT  = [sb("ptT0", A_PTT0, [128, 2048], BF16), sb("ptT1", A_PTT1, [128, 2048], BF16)]
    ident = sb("ident_sb", A_IDENT, [128, 128], BF16)
    maskc = sb("mask_sb", A_MASK, [128, 128], F32)
    ones = sb("ones_sb", A_ONES, [128, 128], BF16)
    svec = sb("svec_sb", A_SVEC, [128, SV_COLS], F32)
    nmax = sb("nmax", A_NMAX, [128, 2], F32)
    recrow = sb("recrow", A_RECR, [1, 256], F32)
    rech = sb("rech", A_RECH, [1, 256], BF16)
    recl = sb("recl", A_RECL, [1, 256], BF16)
    Vp = [sb("Vp0", A_VP0, [128, 1040], BF16), sb("Vp1", A_VP1, [128, 1040], BF16)]
    rrep = sb("rrep", A_RREP, [64, 256], BF16)
    czero = sb("czero", A_CZERO, [128, 8], F32)
    silu = sb("silu_sb", A_SILU, [128, 1024], BF16)
    fftmp = sb("fftmp", A_FFTMP, [128, 512], F32)

    nc.const_aps.register(F32, 0.0, czero[:, 0:1])

    pB0 = E(nc.psum_tensor("pB0", [128, 512], F32))
    pB1 = E(nc.psum_tensor("pB1", [128, 512], F32))
    psg = {}
    psh = {}

    def ps_alloc(name, shape, dtype):
        g = nc.psum_tensor(name + str(nc.next_id()), shape, dtype)
        psg[name] = g
        psh[name] = g.__enter__()
        return psh[name]

    def ps_free(*names):
        for n in names:
            psg.pop(n).__exit__(None, None, None)
            psh.pop(n)

    ps_alloc("ptr0", [128, 1024], BF16)
    ps_alloc("ptr1", [128, 1024], BF16)
    cur = {"pA": ps_alloc("pA", [128, 2048], F32)}

    sem = {k: E(nc.semaphore(k)) for k in ("dW", "dG", "V", "S", "P", "G")}
    cnt = {k: 0 for k in sem}
    ops = {k: [] for k in ("t", "v", "s", "q", "g")}
    lastw = {e: {k: 0 for k in sem} for e in ops}
    marks = []
    nwaits = {e: 0 for e in ops}

    def mark(name):
        marks.append((name, {e: len(ops[e]) for e in ops},
                      {e: nwaits[e] for e in ops}))

    def emit(e, fn):
        ops[e].append(fn)

    def inc(sname, k=1):
        cnt[sname] += k
        return cnt[sname]

    def wait(e, sname, val):
        if val <= 0 or val <= lastw[e][sname]:
            return
        lastw[e][sname] = val
        nwaits[e] += 1
        emit(e, lambda eng, s=sem[sname], v=val: eng.wait_ge(s, v))

    res = {}
    def acquire(e, r):
        for sname, v in res.get(r, ()):
            wait(e, sname, v)
    def release(r, sname, val):
        res[r] = [(sname, val)]
    def release_multi(r, evs):
        res[r] = list(evs)

    def boot():
        v0 = inc("dG")
        emit("g", lambda g: g.dma_start(svec[:, 0:168], dr["svec"]).then_inc(sem["dG"], 16))
        emit("g", lambda g: g.dma_start(ident[:], dr["ident"]).then_inc(sem["dG"], 16))
        emit("g", lambda g: g.dma_start(maskc[:], dr["maskc"]).then_inc(sem["dG"], 16))
        inc("dG"); inc("dG")
        xv = None
        for c in range(NCH):
            xv = inc("dG")
            emit("g", lambda g, c=c: g.dma_start(
                xT[:, c * 2048:(c + 1) * 2048],
                dr["xT"][c * 128:(c + 1) * 128, :]).then_inc(sem["dG"], 16))
        wait("v", "dG", 16 * 3)
        emit("v", lambda v: v.memset(czero[:], 0.0))
        emit("v", lambda v: v.memset(ones[:], 1.0))
        emit("v", lambda v: v.tensor_scalar(
            svec[:, SV_L0SC2:SV_L0SC2 + 8], svec[:, SV_L0SC:SV_L0SC + 8],
            2.0, None, ALU.mult))
        vv = inc("V")
        emit("v", lambda v: v.nop().then_inc(sem["V"], 1))
        release("xT_ready", "dG", 16 * xv)
        release("misc_ready", "V", vv)

    def emit_ln(src_ap, half, tag):
        wait("s", "V", cnt["V"])
        slots = [(pB0, 0), (pB0, 32), (pB0, 64), (pB1, 0)]
        acquire("t", "pB0"); acquire("t", "pB1")
        last_mm = {}
        for c in range(NCH):
            acquire("v", "xbf")
            wait("v", "dG", res.get("xT_ready", [("dG", 0)])[0][1] if tag == "ln_first" else 0)
            vv = inc("V")
            emit("v", lambda v, c=c: v.tensor_copy(xbf[:], src_ap(c)).then_inc(sem["V"], 1))
            acquire("s", "sqbf")
            if tag == "ln_first":
                wait("s", "dG", res.get("xT_ready", [("dG", 0)])[0][1])
            sv = inc("S")
            emit("s", lambda s, c=c: s.activation(sqbf[:], src_ap(c), AF.Square).then_inc(sem["S"], 1))
            wait("t", "V", vv); wait("t", "S", sv)
            for tb in range(2):
                for kind in range(2):
                    pt, row = slots[tb * 2 + kind]
                    srcb = xbf if kind == 0 else sqbf
                    first, lastc = (c == 0), (c == NCH - 1)
                    last_of_chunk = (tb == 1 and kind == 1)
                    if last_of_chunk:
                        pv = inc("P")
                        if lastc:
                            last_mm[(tb, kind)] = pv
                        emit("t", lambda t, pt=pt, row=row, srcb=srcb, tb=tb, first=first, lastc=lastc:
                             t.matmul(pt[row:row + 1, 0:512],
                                      ones[:, 0:1], srcb[:, tb * 512:(tb + 1) * 512],
                                      start=first, stop=lastc).then_inc(sem["P"], 1))
                        release("xbf", "P", pv)
                        release("sqbf", "P", pv)
                    else:
                        if lastc:
                            pv = inc("P"); last_mm[(tb, kind)] = pv
                            emit("t", lambda t, pt=pt, row=row, srcb=srcb, tb=tb, first=first:
                                 t.matmul(pt[row:row + 1, 0:512],
                                          ones[:, 0:1], srcb[:, tb * 512:(tb + 1) * 512],
                                          start=first, stop=True).then_inc(sem["P"], 1))
                        else:
                            emit("t", lambda t, pt=pt, row=row, srcb=srcb, tb=tb, first=first:
                                 t.matmul(pt[row:row + 1, 0:512],
                                          ones[:, 0:1], srcb[:, tb * 512:(tb + 1) * 512],
                                          start=first, stop=False))
        last_mm[(1, 1)] = cnt["P"]
        for tb in range(2):
            pt_s, row_s = slots[tb * 2]
            pt_q, row_q = slots[tb * 2 + 1]
            wait("v", "P", last_mm[(tb, 0)]); wait("v", "P", last_mm[(tb, 1)])
            o = tb * 512
            emit("v", lambda v, pt_s=pt_s, row_s=row_s, o=o: v.tensor_scalar(
                scrm[0:1, o:o + 512], pt_s[row_s:row_s + 1, 0:512], 1.0 / D, None, ALU.mult))
            emit("v", lambda v, o=o: v.tensor_copy(meanb[0:1, o:o + 512], scrm[0:1, o:o + 512]))
            emit("v", lambda v, o=o: v.tensor_mul(scrv[0:1, o:o + 512], scrm[0:1, o:o + 512], scrm[0:1, o:o + 512]))
            emit("v", lambda v, pt_q=pt_q, row_q=row_q, o=o: v.scalar_tensor_tensor(
                scrv[0:1, o:o + 512], pt_q[row_q:row_q + 1, 0:512], 1.0 / D,
                scrv[0:1, o:o + 512], ALU.mult, ALU.subtract))
            vv = inc("V")
            emit("v", lambda v, o=o: v.tensor_scalar(
                scrv[0:1, o:o + 512], scrv[0:1, o:o + 512], 1e-5, None, ALU.add).then_inc(sem["V"], 1))
            wait("s", "V", vv)
            emit("s", lambda s, o=o: s.activation(scrm[0:1, o:o + 512], scrv[0:1, o:o + 512], AF.Ln))
            sv = inc("S")
            emit("s", lambda s, o=o: s.activation(scrv[0:1, o:o + 512], scrm[0:1, o:o + 512],
                                                  AF.Exp, scale=f32(-0.5)).then_inc(sem["S"], 1))
            wait("v", "S", sv)
            emit("v", lambda v, o=o: v.tensor_copy(rsthi[0:1, o:o + 512], scrv[0:1, o:o + 512]))
            vv = inc("V")
            emit("v", lambda v, o=o: v.tensor_sub(rstlo[0:1, o:o + 512], scrv[0:1, o:o + 512],
                                                  rsthi[0:1, o:o + 512]).then_inc(sem["V"], 1))
            if tb == 1:
                release("pB0", "V", vv); release("pB1", "V", vv)
        wait("t", "V", cnt["V"])
        for tb in range(2):
            o = tb * 512
            acquire("t", "pB0"); acquire("t", "pB1")
            pm = inc("P")
            emit("t", lambda t, o=o: t.matmul(pB0[:, 0:512], ones[0:1, 0:128],
                                              meanb[0:1, o:o + 512], start=True, stop=True).then_inc(sem["P"], 1))
            emit("t", lambda t, o=o: t.matmul(pB1[:, 0:512], ones[0:1, 0:128],
                                              rsthi[0:1, o:o + 512], start=True, stop=False))
            pr = inc("P")
            emit("t", lambda t, o=o: t.matmul(pB1[:, 0:512], ones[0:1, 0:128],
                                              rstlo[0:1, o:o + 512], start=False, stop=True).then_inc(sem["P"], 1))
            wait("v", "P", pr)
            vv = None
            for c in range(NCH):
                emit("v", lambda v, c=c, o=o: v.tensor_sub(
                    xhtmp[:], src_ap(c)[:, o:o + 512], pB0[:, 0:512]))
                vv = inc("V")
                emit("v", lambda v, c=c, o=o: v.tensor_mul(
                    xhat[:, c * 1024 + o: c * 1024 + o + 512], xhtmp[:], pB1[:, 0:512]).then_inc(sem["V"], 1))
            release("pB0", "V", vv); release("pB1", "V", vv)
        release("xhat_ready", "V", cnt["V"])
        release("Dreg", "V", cnt["V"])

    def linear_pass(W, G, IC, NTB, rhs_fn, evac_fn, pAx, cast="copy", scale_groups=(),
                    rhs_ready=None, slot_base=0, nslots=4):
        NP = max(1, (IC * 128) // 1024)
        PI = IC // NP
        for g in range(G):
            b = g % 2
            cast_done = None
            for pi in range(NP):
                wl = (g * NP + pi) % 2
                if g == 0 and pi == 0:
                    acquire("q", "Dreg")
                acquire("q", "wld%d" % wl)
                dv = inc("dW")
                emit("q", lambda q, W=W, g=g, pi=pi, wl=wl, PI=PI:
                     q.dma_start(wld[wl][:, 0:PI * 128].rearrange("p (a c) -> p a c", c=128),
                                 W[g, :, pi * PI:(pi + 1) * PI, :]).then_inc(sem["dW"], 16))
                wait("v", "dW", 16 * dv)
                if pi == 0:
                    acquire("v", "wbf%d" % b)
                dst = lambda: wbf[b][:, pi * PI * 128:(pi + 1) * PI * 128]
                cast_done = inc("V")
                if cast == "copy" or (cast == "scale" and g not in scale_groups):
                    emit("v", lambda v, b=b, pi=pi, PI=PI, wl=wl: v.tensor_copy(
                        wbf[b][:, pi * PI * 128:(pi + 1) * PI * 128],
                        wld[wl][:, 0:PI * 128]).then_inc(sem["V"], 1))
                elif cast == "scale":
                    emit("v", lambda v, b=b, pi=pi, PI=PI, wl=wl: v.tensor_scalar(
                        wbf[b][:, pi * PI * 128:(pi + 1) * PI * 128],
                        wld[wl][:, 0:PI * 128], 0.125, None, ALU.mult).then_inc(sem["V"], 1))
                elif cast == "bin":
                    emit("v", lambda v, b=b, pi=pi, PI=PI, wl=wl: v.tensor_scalar(
                        wbf[b][:, pi * PI * 128:(pi + 1) * PI * 128],
                        wld[wl][:, 0:PI * 128], 0.0, 0.5, ALU.is_ge, ALU.subtract).then_inc(sem["V"], 1))
                release("wld%d" % wl, "V", cast_done)
            wait("t", "V", cast_done)
            if rhs_ready is not None and g == 0:
                wait("t", rhs_ready[0], rhs_ready[1])
            last_of_group = None
            for tb in range(NTB):
                slot = slot_base + (g * NTB + tb) % nslots
                acquire("t", "pA%d" % slot)
                for ic in range(IC):
                    first, last = (ic == 0), (ic == IC - 1)
                    if last:
                        pv = inc("P"); last_of_group = pv
                        emit("t", lambda t, b=b, ic=ic, slot=slot, tb=tb, first=first:
                             t.matmul(pAx[:, slot * 512:(slot + 1) * 512],
                                      wbf[b][:, ic * 128:(ic + 1) * 128],
                                      rhs_fn(ic, tb), start=first, stop=True).then_inc(sem["P"], 1))
                    else:
                        emit("t", lambda t, b=b, ic=ic, slot=slot, tb=tb, first=first:
                             t.matmul(pAx[:, slot * 512:(slot + 1) * 512],
                                      wbf[b][:, ic * 128:(ic + 1) * 128],
                                      rhs_fn(ic, tb), start=first, stop=False))
                evac_fn(g, tb, slot, cnt["P"])
            release("wbf%d" % b, "P", last_of_group)
            if g == G - 1:
                release_multi("Dreg", [("P", last_of_group), ("V", cast_done)])

    def mk_evac_copy(dst_fn, pAx):
        def ev(g, tb, slot, mm_done):
            wait("s", "P", mm_done)
            sv = inc("S")
            emit("s", lambda s, g=g, tb=tb, slot=slot: s.copy(
                dst_fn(g, tb), pAx[:, slot * 512:(slot + 1) * 512]).then_inc(sem["S"], 1))
            release("pA%d" % slot, "S", sv)
        return ev

    def mk_evac_resid(bout_col, pAx):
        def ev(g, tb, slot, mm_done):
            wait("v", "P", mm_done)
            vv = inc("V")
            emit("v", lambda v, g=g, tb=tb, slot=slot: v.scalar_tensor_tensor(
                xT[:, g * 2048 + tb * 512: g * 2048 + (tb + 1) * 512],
                pAx[:, slot * 512:(slot + 1) * 512],
                svec[:, bout_col + g: bout_col + g + 1],
                xT[:, g * 2048 + tb * 512: g * 2048 + (tb + 1) * 512],
                ALU.add, ALU.add).then_inc(sem["V"], 1))
            release("pA%d" % slot, "V", vv)
        return ev

    def emit_attn(li):
        ptr = [psh["ptr0"], psh["ptr1"]]
        pA = cur["pA"]
        slot_rr = [0]
        wait("g", "P", cnt["P"])
        wait("g", "V", cnt["V"])
        for hb in range(2):
            emit("g", lambda g, hb=hb: g.memset(
                Vp[hb][:].rearrange("p (a c) -> p a c", c=65)[:, :, 64:65], 1.0))
        gv = inc("G")
        emit("g", lambda g: g.nop().then_inc(sem["G"], 1))
        release("vp_ones", "G", gv)

        for h in range(NH):
            hp = 64 * (h % 2)
            hb = h % 2
            qc = (h // 2) * 2048
            kc = (8 + h // 2) * 2048
            vc = (16 + h // 2) * 2048
            acquire("t", "ptr%d" % hb)
            pv = None
            for kt in range(16):
                last = (kt == 15)
                if last:
                    pv = inc("P")
                    emit("t", lambda t, vc=vc, hp=hp, kt=kt, hb=hb, ptr=ptr: t.transpose(
                        ptr[hb][:, kt * 64:(kt + 1) * 64],
                        qkvT[hp:hp + 64, vc + kt * 128: vc + (kt + 1) * 128],
                        ident[hp:hp + 64, hp:hp + 64]).then_inc(sem["P"], 1))
                else:
                    emit("t", lambda t, vc=vc, hp=hp, kt=kt, hb=hb, ptr=ptr: t.transpose(
                        ptr[hb][:, kt * 64:(kt + 1) * 64],
                        qkvT[hp:hp + 64, vc + kt * 128: vc + (kt + 1) * 128],
                        ident[hp:hp + 64, hp:hp + 64]))
            acquire("s", "Vp%d" % hb)
            acquire("s", "vp_ones")
            if h < 2:
                acquire("s", "Dreg")
            wait("s", "P", pv)
            sv = inc("S")
            emit("s", lambda s, hb=hb, ptr=ptr: s.copy(
                Vp[hb][:].rearrange("p (a c) -> p a c", c=65)[:, :, 0:64],
                ptr[hb][:, 0:1024].rearrange("p (a c) -> p a c", c=64)).then_inc(sem["S"], 1))
            release("ptr%d" % hb, "S", sv)
            vp_ready = sv

            for qt in range(16):
                p = qt % 2
                q0 = qt * 128
                nval = q0 + 128
                nkb = (nval + 511) // 512
                sc_mm = []
                slots = []
                for kb in range(nkb):
                    N = min(512, nval - kb * 512)
                    slot = slot_rr[0] % 4
                    slot_rr[0] += 1
                    slots.append(slot)
                    acquire("t", "pA%d" % slot)
                    pvm = inc("P")
                    sc_mm.append(pvm)
                    emit("t", lambda t, hp=hp, qc=qc, kc=kc, q0=q0, kb=kb, N=N, slot=slot, pA=pA:
                         t.matmul(pA[0:128, slot * 512: slot * 512 + N],
                                  qkvT[hp:hp + 64, qc + q0: qc + q0 + 128],
                                  qkvT[hp:hp + 64, kc + kb * 512: kc + kb * 512 + N],
                                  start=True, stop=True).then_inc(sem["P"], 1))
                wait("v", "P", sc_mm[-1])
                dslot = slots[-1]
                dN = min(512, nval - (nkb - 1) * 512)
                mv = inc("V")
                emit("v", lambda v, dslot=dslot, dN=dN, pA=pA: v.scalar_tensor_tensor(
                    pA[:, dslot * 512 + dN - 128: dslot * 512 + dN],
                    pA[:, dslot * 512 + dN - 128: dslot * 512 + dN],
                    1.0, maskc[:], ALU.mult, ALU.add).then_inc(sem["V"], 1))
                acquire("s", "S%d" % p)
                evs = None
                for kb in range(nkb):
                    N = min(512, nval - kb * 512)
                    wait("s", "P", sc_mm[kb])
                    if kb == nkb - 1:
                        wait("s", "V", mv)
                    evs = inc("S")
                    emit("s", lambda s, p=p, kb=kb, N=N, slot=slots[kb], pA=pA: s.copy(
                        S_sb[p][:, kb * 512: kb * 512 + N],
                        pA[:, slot * 512: slot * 512 + N]).then_inc(sem["S"], 1))
                    release("pA%d" % slots[kb], "S", evs)
                acquire("v", "nmax%d" % p)
                wait("v", "S", evs)
                vv = inc("V")
                emit("v", lambda v, p=p, nval=nval: v.tensor_reduce(
                    nmax[:, p:p + 1], S_sb[p][:, 0:nval], AX.X, ALU.max,
                    negate=True).then_inc(sem["V"], 1))
                acquire("s", "P%d" % p)
                wait("s", "V", vv)
                sv = inc("S")
                emit("s", lambda s, p=p, nval=nval: s.activation(
                    P_sb[p][:, 0:nval], S_sb[p][:, 0:nval], AF.Exp,
                    bias=nmax[:, p:p + 1], scale=f32(1.0)).then_inc(sem["S"], 1))
                release("S%d" % p, "S", sv)
                release("nmax%d" % p, "S", sv)
                nkt = qt + 1
                wait("t", "S", sv)
                tr_done = []
                for bank in range((nkt + 7) // 8):
                    acquire("t", "ptr%d" % bank)
                    hi = min(nkt, bank * 8 + 8)
                    for kt in range(bank * 8, hi):
                        last = (kt == hi - 1)
                        if last:
                            pvt = inc("P"); tr_done.append(pvt)
                            emit("t", lambda t, p=p, kt=kt, bank=bank, ptr=ptr: t.transpose(
                                ptr[bank][:, (kt % 8) * 128:(kt % 8 + 1) * 128],
                                P_sb[p][:, kt * 128:(kt + 1) * 128],
                                ident[:]).then_inc(sem["P"], 1))
                        else:
                            emit("t", lambda t, p=p, kt=kt, bank=bank, ptr=ptr: t.transpose(
                                ptr[bank][:, (kt % 8) * 128:(kt % 8 + 1) * 128],
                                P_sb[p][:, kt * 128:(kt + 1) * 128],
                                ident[:]))
                release("P%d" % p, "P", tr_done[-1])
                acquire("s", "ptT%d" % p)
                ptev = []
                for bank in range((nkt + 7) // 8):
                    hi = min(nkt, bank * 8 + 8)
                    n = (hi - bank * 8) * 128
                    wait("s", "P", tr_done[bank])
                    sv = inc("S")
                    ptev.append(sv)
                    emit("s", lambda s, p=p, bank=bank, n=n, ptr=ptr: s.copy(
                        ptT[p][:, bank * 1024: bank * 1024 + n],
                        ptr[bank][:, 0:n]).then_inc(sem["S"], 1))
                    release("ptr%d" % bank, "S", sv)
                pvb = pB0 if p == 0 else pB1
                acquire("t", "pB%d" % p)
                wait("t", "S", max(ptev[-1], vp_ready))
                pvv = None
                for kt in range(nkt):
                    last = (kt == nkt - 1)
                    if last:
                        pvv = inc("P")
                        emit("t", lambda t, p=p, kt=kt, hb=hb, pvb=pvb: t.matmul(
                            pvb[0:65, 0:128], Vp[hb][:, kt * 65: kt * 65 + 65],
                            ptT[p][:, kt * 128:(kt + 1) * 128],
                            start=(kt == 0), stop=True).then_inc(sem["P"], 1))
                    else:
                        emit("t", lambda t, p=p, kt=kt, hb=hb, pvb=pvb: t.matmul(
                            pvb[0:65, 0:128], Vp[hb][:, kt * 65: kt * 65 + 65],
                            ptT[p][:, kt * 128:(kt + 1) * 128],
                            start=(kt == 0), stop=False))
                release("ptT%d" % p, "P", pvv)
                if qt == 15:
                    release("Vp%d" % hb, "P", pvv)
                wait("v", "P", pvv)
                emit("v", lambda v, p=p, pvb=pvb: v.reciprocal(
                    recrow[0:1, p * 128:(p + 1) * 128], pvb[64:65, 0:128]))
                emit("v", lambda v, p=p: v.tensor_copy(
                    rech[0:1, p * 128:(p + 1) * 128], recrow[0:1, p * 128:(p + 1) * 128]))
                vv = inc("V")
                emit("v", lambda v, p=p: v.tensor_sub(
                    recl[0:1, p * 128:(p + 1) * 128], recrow[0:1, p * 128:(p + 1) * 128],
                    rech[0:1, p * 128:(p + 1) * 128]).then_inc(sem["V"], 1))
                rb = pB1 if p == 0 else pB0
                acquire("t", "pB%d" % (1 - p))
                wait("t", "V", vv)
                emit("t", lambda t, p=p, rb=rb: t.matmul(
                    rb[0:64, 0:128], ones[0:1, 0:64], rech[0:1, p * 128:(p + 1) * 128],
                    start=True, stop=False))
                pvr = inc("P")
                emit("t", lambda t, p=p, rb=rb: t.matmul(
                    rb[0:64, 0:128], ones[0:1, 0:64], recl[0:1, p * 128:(p + 1) * 128],
                    start=False, stop=True).then_inc(sem["P"], 1))
                wait("v", "P", pvr)
                vv0 = inc("V")
                emit("v", lambda v, p=p, rb=rb: v.tensor_copy(
                    rrep[0:64, p * 128:(p + 1) * 128], rb[0:64, 0:128]).then_inc(sem["V"], 1))
                release("pB%d" % (1 - p), "V", vv0)
                vv = inc("V")
                emit("v", lambda v, hp=hp, qc=qc, q0=q0, p=p, pvb=pvb: v.tensor_mul(
                    qkvT[hp:hp + 64, qc + q0: qc + q0 + 128],
                    pvb[0:64, 0:128], rrep[0:64, p * 128:(p + 1) * 128]).then_inc(sem["V"], 1))
                release("pB%d" % p, "V", vv)
        release("attn_done", "V", cnt["V"])
        release("Dreg", "P", cnt["P"])

    def w1w2_pass(half, pAx):
        G, IC = 32, 8
        for g in range(G):
            b = g % 2
            cast_done = None
            for wi, W in enumerate((dr["w1"], dr["w2"])):
                wl = (g * 2 + wi) % 2
                if g == 0 and wi == 0:
                    acquire("q", "Dreg")
                acquire("q", "wld%d" % wl)
                dv = inc("dW")
                emit("q", lambda q, W=W, g=g, wl=wl: q.dma_start(
                    wld[wl][:, 0:1024].rearrange("p (a c) -> p a c", c=128),
                    W[g, :, :, :]).then_inc(sem["dW"], 16))
                wait("v", "dW", 16 * dv)
                if wi == 0:
                    acquire("v", "wbf%d" % b)
                cast_done = inc("V")
                emit("v", lambda v, b=b, wi=wi, wl=wl: v.tensor_copy(
                    wbf[b][:, wi * 1024:(wi + 1) * 1024], wld[wl][:, 0:1024]).then_inc(sem["V"], 1))
                release("wld%d" % wl, "V", cast_done)
            wait("t", "V", cast_done)
            last_mm = None
            for tb in range(2):
                s1, s2 = tb, 2 + tb
                acquire("t", "pA%d" % s1); acquire("t", "pA%d" % s2)
                for ic in range(IC):
                    last = ic == IC - 1
                    if last:
                        p1 = inc("P")
                        emit("t", lambda t, b=b, ic=ic, s1=s1, tb=tb:
                             t.matmul(pAx[:, s1 * 512:(s1 + 1) * 512], wbf[b][:, ic * 128:(ic + 1) * 128],
                                      xhat[:, ic * 1024 + tb * 512: ic * 1024 + (tb + 1) * 512],
                                      start=(ic == 0), stop=True).then_inc(sem["P"], 1))
                    else:
                        emit("t", lambda t, b=b, ic=ic, s1=s1, tb=tb:
                             t.matmul(pAx[:, s1 * 512:(s1 + 1) * 512], wbf[b][:, ic * 128:(ic + 1) * 128],
                                      xhat[:, ic * 1024 + tb * 512: ic * 1024 + (tb + 1) * 512],
                                      start=(ic == 0), stop=False))
                for ic in range(IC):
                    last = ic == IC - 1
                    if last:
                        p2 = inc("P"); last_mm = p2
                        emit("t", lambda t, b=b, ic=ic, s2=s2, tb=tb:
                             t.matmul(pAx[:, s2 * 512:(s2 + 1) * 512], wbf[b][:, 1024 + ic * 128: 1024 + (ic + 1) * 128],
                                      xhat[:, ic * 1024 + tb * 512: ic * 1024 + (tb + 1) * 512],
                                      start=(ic == 0), stop=True).then_inc(sem["P"], 1))
                    else:
                        emit("t", lambda t, b=b, ic=ic, s2=s2, tb=tb:
                             t.matmul(pAx[:, s2 * 512:(s2 + 1) * 512], wbf[b][:, 1024 + ic * 128: 1024 + (ic + 1) * 128],
                                      xhat[:, ic * 1024 + tb * 512: ic * 1024 + (tb + 1) * 512],
                                      start=(ic == 0), stop=False))
                wait("s", "P", p1)
                acquire("s", "silu%d" % tb)
                sv = inc("S")
                emit("s", lambda s, g=g, tb=tb, s1=s1: s.activation(
                    silu[:, tb * 512:(tb + 1) * 512], pAx[:, s1 * 512:(s1 + 1) * 512],
                    AF.Silu, scale=svec[:, SV_W1S + g: SV_W1S + g + 1]).then_inc(sem["S"], 1))
                release("pA%d" % s1, "S", sv)
                wait("v", "P", p2); wait("v", "S", sv)
                vv = inc("V")
                emit("v", lambda v, g=g, tb=tb, s2=s2: v.scalar_tensor_tensor(
                    h_sb[:, g * 1024 + tb * 512: g * 1024 + (tb + 1) * 512],
                    pAx[:, s2 * 512:(s2 + 1) * 512],
                    svec[:, SV_W2S + g: SV_W2S + g + 1],
                    silu[:, tb * 512:(tb + 1) * 512], ALU.mult, ALU.mult).then_inc(sem["V"], 1))
                release("pA%d" % s2, "V", vv)
                release("silu%d" % tb, "V", vv)
            release("wbf%d" % b, "P", last_mm)
            if g == G - 1:
                release_multi("Dreg", [("P", last_mm), ("V", cast_done)])

    boot()
    xsrc = lambda c: xT[:, c * 2048:(c + 1) * 2048]

    STOPPED = [False]

    def dump_and_stop(src_fn):
        wait("g", "V", cnt["V"]); wait("g", "S", cnt["S"]); wait("g", "P", cnt["P"])
        for c in range(NCH):
            inc("dG")
            emit("g", lambda g, c=c: g.dma_start(
                outT[c * 128:(c + 1) * 128, :], src_fn(c)).then_inc(sem["dG"], 16))
        STOPPED[0] = True

    def xsrc_half(half):
        return lambda c: xT[:, c * 2048 + half * 1024: c * 2048 + half * 1024 + 1024]

    def qkv_dst(base_chunk, half):
        return lambda g, tb: qkvT[:, (base_chunk + g) * 2048 + half * 1024 + tb * 512:
                                   (base_chunk + g) * 2048 + half * 1024 + (tb + 1) * 512]

    if stop_after == "boot":
        dump_and_stop(xsrc)

    for li in range(2):
        if STOPPED[0]:
            break
        for half in range(2):
            mark("L%d_attn_ln%d" % (li, half))
            emit_ln(xsrc_half(half), half, "ln_first" if (li == 0 and half == 0) else "ln")
            mark("L%d_attn_qkv%d" % (li, half))
            if stop_after == ("ln%d" % li) and half == 1:
                wait("g", "V", cnt["V"]); wait("g", "S", cnt["S"]); wait("g", "P", cnt["P"])
                for c in range(NCH):
                    inc("dG")
                    emit("g", lambda g, c=c: g.dma_start(
                        outT[c * 128:(c + 1) * 128, 0:1024],
                        xhat[:, c * 1024:(c + 1) * 1024]).then_inc(sem["dG"], 16))
                STOPPED[0] = True
                break
            xr = ("V", cnt["V"])
            rhs = lambda ic, tb: xhat[:, ic * 1024 + tb * 512: ic * 1024 + (tb + 1) * 512]
            if li == 0:
                linear_pass(dr["wq0"], 8, 8, 2, rhs, mk_evac_copy(qkv_dst(0, half), cur["pA"]), cur["pA"],
                            cast="scale", scale_groups=set(range(8)), rhs_ready=xr)
                linear_pass(dr["wk0"], 8, 8, 2, rhs, mk_evac_copy(qkv_dst(8, half), cur["pA"]), cur["pA"], rhs_ready=xr)
                linear_pass(dr["wv0"], 8, 8, 2, rhs, mk_evac_copy(qkv_dst(16, half), cur["pA"]), cur["pA"], rhs_ready=xr)
            else:
                linear_pass(dr["wqkv1"], 24, 8, 2, rhs, mk_evac_copy(qkv_dst(0, half), cur["pA"]), cur["pA"],
                            cast="scale", scale_groups=set(range(8)), rhs_ready=xr)
        if STOPPED[0]:
            break
        if stop_after in ("q%d" % li, "k%d" % li, "v%d" % li):
            base = {"q": 0, "k": 8, "v": 16}[stop_after[0]]
            dump_and_stop(lambda c: qkvT[:, (base + c) * 2048:(base + c + 1) * 2048])
            break
        wait("t", "S", cnt["S"])
        mark("L%d_attn_core" % li)
        emit_attn(li)
        if stop_after == ("att%d" % li):
            dump_and_stop(lambda c: qkvT[:, c * 2048:(c + 1) * 2048])
            break
        mark("L%d_wout" % li)
        orhs = lambda ic, tb: qkvT[:, ic * 2048 + tb * 512: ic * 2048 + (tb + 1) * 512]
        wname = "wo0" if li == 0 else "wo1"
        bout_col = SV_BOUT0 if li == 0 else SV_BOUT1
        linear_pass(dr[wname], 8, 8, 4, orhs, mk_evac_resid(bout_col, cur["pA"]), cur["pA"],
                    rhs_ready=("V", cnt["V"]))
        if stop_after == ("l%da" % li):
            break

        if li == 0:
            for half in range(2):
                if stop_after == "skipffn0":
                    break
                mark("L0_ffn%d" % half)
                emit_ln(xsrc_half(half), half, "ln")
                def ev(g, tb, slot, mm_done, half=half, pAx=cur["pA"]):
                    wait("v", "P", mm_done)
                    emit("v", lambda v, g=g, slot=slot, pAx=pAx: v.tensor_scalar(
                        fftmp[:], pAx[:, slot * 512:(slot + 1) * 512],
                        svec[:, SV_L0SC2 + g: SV_L0SC2 + g + 1],
                        svec[:, SV_L0BI + g: SV_L0BI + g + 1], ALU.mult, ALU.add))
                    emit("v", lambda v: v.tensor_scalar(
                        fftmp[:], fftmp[:], 100.0, -100.0, ALU.min, ALU.max))
                    vv = inc("V")
                    emit("v", lambda v, g=g, tb=tb, half=half: v.scalar_tensor_tensor(
                        xT[:, g * 2048 + half * 1024 + tb * 512: g * 2048 + half * 1024 + (tb + 1) * 512],
                        fftmp[:], 1.0,
                        xT[:, g * 2048 + half * 1024 + tb * 512: g * 2048 + half * 1024 + (tb + 1) * 512],
                        ALU.mult, ALU.add).then_inc(sem["V"], 1))
                    release("pA%d" % slot, "V", vv)
                rhs = lambda ic, tb: xhat[:, ic * 1024 + tb * 512: ic * 1024 + (tb + 1) * 512]
                linear_pass(dr["wf0"], 8, 8, 2, rhs, ev, cur["pA"], cast="bin", rhs_ready=("V", cnt["V"]))
            if stop_after == "l0f":
                break
        else:
            for half in range(2):
                if half == 0:
                    emit_ln(xsrc_half(0), 0, "ln")
                    wait("g", "V", cnt["V"])
                    spv = None
                    for c in range(NCH):
                        spv = inc("dG")
                        emit("g", lambda g, c=c: g.dma_start(
                            xsp[c * 128:(c + 1) * 128, :], xT[:, c * 2048:(c + 1) * 2048]).then_inc(sem["dG"], 16))
                    release("spill_done", "dG", 16 * spv)
                else:
                    acquire("g", "xrel")
                    wait("g", "V", cnt["V"])
                    rl = None
                    for c in range(NCH):
                        rl = inc("dG")
                        emit("g", lambda g, c=c: g.dma_start(
                            xrel[:, c * 1024:(c + 1) * 1024],
                            xsp[c * 128:(c + 1) * 128, 1024:2048]).then_inc(sem["dG"], 16))
                    wait("v", "dG", 16 * rl)
                    wait("s", "dG", 16 * rl)
                    emit_ln(lambda c: xrel[:, c * 1024:(c + 1) * 1024], 1, "ln")
                for s_, v_ in res["spill_done"]:
                    wait("v", s_, v_); wait("s", s_, v_)
                mark("L1_ffn_w12_%d" % half)
                w1w2_pass(half, cur["pA"])
                def gelu_evac(dst, scol, pAx=None):
                    pAx = pAx or cur["pA"]
                    def ev(g, tb, slot, mm_done, pAx=pAx):
                        wait("s", "P", mm_done)
                        sv = inc("S")
                        emit("s", lambda s, g=g, tb=tb, slot=slot, pAx=pAx: s.activation(
                            dst(g, tb), pAx[:, slot * 512:(slot + 1) * 512], AF.Gelu,
                            scale=svec[:, scol + g: scol + g + 1]).then_inc(sem["S"], 1))
                        release("pA%d" % slot, "S", sv)
                    return ev
                hrhs = lambda ic, tb: h_sb[:, ic * 1024 + tb * 512: ic * 1024 + (tb + 1) * 512]
                grhs = lambda ic, tb: g_sb[:, ic * 1024 + tb * 512: ic * 1024 + (tb + 1) * 512]
                gdst = lambda g, tb: g_sb[:, g * 1024 + tb * 512: g * 1024 + (tb + 1) * 512]
                fdst = lambda g, tb: h_sb[:, g * 1024 + tb * 512: g * 1024 + (tb + 1) * 512]
                mark("L1_ffn_h1_%d" % half)
                linear_pass(dr["h1"], 32, 32, 2, hrhs, gelu_evac(gdst, SV_H1S),
                            cur["pA"], rhs_ready=("V", cnt["V"]))
                mark("L1_ffn_h2_%d" % half)
                linear_pass(dr["h2"], 32, 32, 2, grhs, gelu_evac(fdst, SV_H2S),
                            cur["pA"], rhs_ready=("S", cnt["S"]))
                mark("L1_ffn_w3_%d" % half)
                if half == 0:
                    acquire("g", "xrel")
                    for s_, v_ in res["spill_done"]:
                        wait("g", s_, v_)
                    wait("g", "P", cnt["P"])
                    rl = None
                    for c in range(NCH):
                        rl = inc("dG")
                        emit("g", lambda g, c=c: g.dma_start(
                            xrel[:, c * 1024:(c + 1) * 1024],
                            xsp[c * 128:(c + 1) * 128, 0:1024]).then_inc(sem["dG"], 16))
                    wait("v", "dG", 16 * rl)
                out_done = {}
                def w3_evac(g, tb, slot, mm_done, half=half, pAx=cur["pA"]):
                    wait("v", "P", mm_done)
                    vv = inc("V")
                    emit("v", lambda v, g=g, tb=tb, slot=slot, pAx=pAx: v.scalar_tensor_tensor(
                        xrel[:, g * 1024 + tb * 512: g * 1024 + (tb + 1) * 512],
                        pAx[:, slot * 512:(slot + 1) * 512],
                        svec[:, SV_W3S + g: SV_W3S + g + 1],
                        xrel[:, g * 1024 + tb * 512: g * 1024 + (tb + 1) * 512],
                        ALU.mult, ALU.add).then_inc(sem["V"], 1))
                    release("pA%d" % slot, "V", vv)
                    out_done[(g, tb)] = vv
                frhs = lambda ic, tb: h_sb[:, ic * 1024 + tb * 512: ic * 1024 + (tb + 1) * 512]
                linear_pass(dr["w3"], 8, 32, 2, frhs, w3_evac, cur["pA"], rhs_ready=("S", cnt["S"]))
                ov = None
                for c in range(NCH):
                    wait("g", "V", out_done[(c, 1)])
                    ov = inc("dG")
                    emit("g", lambda g, c=c, half=half: g.dma_start(
                        outT[c * 128:(c + 1) * 128, half * 1024:(half + 1) * 1024],
                        xrel[:, c * 1024:(c + 1) * 1024]).then_inc(sem["dG"], 16))
                release("xrel", "dG", 16 * ov)

        if stop_after == "l1a" and li == 1:
            break

    if stop_after != "full" and not STOPPED[0]:
        wait("g", "V", cnt["V"])
        wait("g", "S", cnt["S"])
        for c in range(NCH):
            inc("dG")
            emit("g", lambda g, c=c: g.dma_start(
                outT[c * 128:(c + 1) * 128, :], xT[:, c * 2048:(c + 1) * 2048]).then_inc(sem["dG"], 16))

    wait("g", "dG", 16 * cnt["dG"])
    wait("q", "dW", 16 * cnt["dW"])

    with nc.Block() as block:
        @block.sync
        def _(q):
            for fn in ops["q"]:
                fn(q)

        @block.gpsimd
        def _(g):
            for fn in ops["g"]:
                fn(g)

        @block.vector
        def _(v):
            for fn in ops["v"]:
                fn(v)

        @block.scalar
        def _(s):
            for fn in ops["s"]:
                fn(s)

        @block.tensor
        def _(t):
            for fn in ops["t"]:
                fn(t)

    ps_free("pA", "ptr1", "ptr0")
    ctx.close()
    nc._phase_marks = marks
    n = sum(len(v) for v in ops.values())
    return nc, n


_NC_CACHE = {}


def _get_nc(stop_after="full"):
    if stop_after not in _NC_CACHE:
        _NC_CACHE[stop_after] = build_nc(stop_after)[0]
    return _NC_CACHE[stop_after]


def _prep_inputs(inputs):
    mask = np.where(np.arange(128)[None, :] <= np.arange(128)[:, None], 0.0, NEG).astype(np.float32)
    svec = np.zeros((128, SV_COLS - 8), dtype=np.float32)
    svec[:, SV_L0SC:SV_L0SC + 8] = _svpack(inputs["L0_ffn_scale"])
    svec[:, SV_L0BI:SV_L0BI + 8] = _svpack(inputs["L0_ffn_bias"])
    svec[:, SV_BOUT1:SV_BOUT1 + 8] = _svpack(inputs["L1_attn_bout"])
    svec[:, SV_W1S:SV_W1S + 32] = _svpack(inputs["L1_ffn_w1_s"])
    svec[:, SV_W2S:SV_W2S + 32] = _svpack(inputs["L1_ffn_w2_s"])
    svec[:, SV_H1S:SV_H1S + 32] = _svpack(inputs["L1_ffn_h1_s"])
    svec[:, SV_H2S:SV_H2S + 32] = _svpack(inputs["L1_ffn_h2_s"])
    svec[:, SV_W3S:SV_W3S + 8] = _svpack(inputs["L1_ffn_w3_s"])
    common = {
        "wq0": _blk(inputs["L0_attn_W_Q"]), "wk0": _blk(inputs["L0_attn_W_K"]),
        "wv0": _blk(inputs["L0_attn_W_V"]), "wo0": _blk(inputs["L0_attn_W_O"]),
        "wf0": _blk(inputs["L0_ffn_weights"]),
        "wqkv1": _blk(inputs["L1_attn_Wqkv"]), "wo1": _blk(inputs["L1_attn_Wout"]),
        "w1": _blk(inputs["L1_ffn_w1_w"]), "w2": _blk(inputs["L1_ffn_w2_w"]),
        "h1": _blk(inputs["L1_ffn_h1_w"]), "h2": _blk(inputs["L1_ffn_h2_w"]),
        "w3": _blk(inputs["L1_ffn_w3_w"]),
        "svec": np.ascontiguousarray(svec),
        "ident": np.eye(128, dtype=ml_dtypes.bfloat16),
        "maskc": mask,
    }
    x = np.asarray(inputs["L0_attn_x"])
    in_maps = []
    for b in range(8):
        m = dict(common)
        m["xT"] = np.ascontiguousarray(x[b].T.astype(np.float32))
        in_maps.append(m)
    return in_maps


def kernel(**inputs):
    in_maps = _prep_inputs(inputs)
    nc = _get_nc("full")
    res = run_bass_kernel_spmd(nc, in_maps, core_ids=list(range(8)))
    out = np.stack([np.asarray(r["outT"]).T for r in res.results], axis=0)
    return np.ascontiguousarray(out.astype(np.float32))


# version 04
# speedup vs baseline: 1.2771x; 1.2771x over previous
import numpy as np
import ml_dtypes
import concourse.bass as bass
import concourse.mybir as mybir
from concourse.bass_utils import run_bass_kernel_spmd

F32 = mybir.dt.float32
BF16 = mybir.dt.bfloat16
AF = mybir.ActivationFunctionType
ALU = mybir.AluOpType
AX = mybir.AxisListType
f32 = np.float32

T = 2048
HT = 1024
D = 1024
NCH = 8
NH = 16
DH = 64
DHID = 4096
NEG = -1.0e30

A_XT    = 0
A_QKVT  = 65536
A_G     = 65536
A_XREL  = 131072
A_XHAT  = 163840
A_PT0   = 163840
A_PT1   = 164864
A_PT2   = 165888
A_WLD0  = 180224
A_WLD1  = 184320
A_WBF0  = 188416
A_WBF1  = 196608
A_XBF   = 180224
A_SQBF  = 182272
A_MEANB = 184320
A_RSTHI = 186368
A_RSTLO = 188416
A_SCRM  = 190464
A_SCRV  = 194560
A_XHTMP = 198656
A_IDENT = 204800
A_ONES  = 205568
A_SVEC  = 205824
A_CZERO = 206592
A_SILU  = 206656
A_FFTMP = 208704
A_RECR  = 206656
A_REC2H = 198656
A_REC2L = 200704
A_VP0   = 188416
A_VP1   = 190528
A_RREP0 = 192640
A_RREP1 = 194688
A_MSKW  = 210752
ARENA_BYTES = 212800

SV_L0SC = 0
SV_L0BI = 8
SV_BOUT0 = 16
SV_BOUT1 = 24
SV_W1S = 32
SV_W2S = 64
SV_H1S = 96
SV_H2S = 128
SV_W3S = 160
SV_L0SC2 = 168
SV_COLS = 176


def _blk(w):
    O, I = w.shape
    G, IC = O // 128, I // 128
    wt = np.ascontiguousarray(np.asarray(w).T)
    return np.ascontiguousarray(
        wt.reshape(IC, 128, G, 128).transpose(2, 1, 0, 3))


def _svpack(v):
    v = np.asarray(v, dtype=np.float32)
    return np.ascontiguousarray(v.reshape(-1, 128).T)


def build_nc(stop_after="full"):
    nc = bass.Bass("TRN2", target_bir_lowering=False)

    dr = {}
    def din(name, shape, dt=F32):
        dr[name] = nc.dram_tensor(name, shape, dt, kind="ExternalInput").ap()
        return dr[name]

    din("xT", [D, T])
    din("wq0", [8, 128, 8, 128]); din("wk0", [8, 128, 8, 128])
    din("wv0", [8, 128, 8, 128]); din("wo0", [8, 128, 8, 128])
    din("wf0", [8, 128, 8, 128])
    din("wqkv1", [24, 128, 8, 128]); din("wo1", [8, 128, 8, 128])
    din("w1", [32, 128, 8, 128]); din("w2", [32, 128, 8, 128])
    din("h1", [32, 128, 32, 128]); din("h2", [32, 128, 32, 128])
    din("w3", [8, 128, 32, 128])
    din("svec", [128, 168])
    din("ident", [128, 128], BF16)
    din("maskc", [128, 896], BF16)
    outT = nc.dram_tensor("outT", [D, T], F32, kind="ExternalOutput").ap()
    xsp = nc.dram_tensor("xspill", [D, T], F32, kind="Internal").ap()

    from contextlib import ExitStack
    ctx = ExitStack()
    E = ctx.enter_context

    backing = E(nc.sbuf_tensor("arena", [128, ARENA_BYTES], mybir.dt.uint8))
    base = nc.lookup_mloc(backing).addr

    def sb(name, off, shape, dtype):
        h = nc.virtual_sbuf_tensor(name, shape, dtype)
        nc._tensor(h.name, shape, dtype, type="SB")
        ml = nc.lookup_mloc(h)
        ml.allocated = True
        ml.addr = base + off
        return h

    xT   = sb("xT_sb",  A_XT,   [128, 16384], F32)
    h_sb = sb("h_sb",   A_XT,   [128, 32768], BF16)
    qkvT = sb("qkvT",   A_QKVT, [128, 49152], BF16)
    g_sb = sb("g_sb",   A_G,    [128, 32768], BF16)
    xrel = sb("xrel",   A_XREL, [128, 8192],  F32)
    xhat = sb("xhat",   A_XHAT, [128, 8192],  BF16)
    pt_sb = [sb("pt%d" % i, A_PT0 + i * 1024, [128, 512], BF16) for i in range(3)]
    maskw = sb("maskw", A_MSKW, [128, 896], BF16)
    wld  = [sb("wld0", A_WLD0, [128, 1024], F32), sb("wld1", A_WLD1, [128, 1024], F32)]
    wbf  = [sb("wbf0", A_WBF0, [128, 4096], BF16), sb("wbf1", A_WBF1, [128, 4096], BF16)]
    xbf  = sb("xbf",   A_XBF,   [128, 1024], BF16)
    sqbf = sb("sqbf",  A_SQBF,  [128, 1024], BF16)
    meanb = sb("meanb", A_MEANB, [1, 1024], BF16)
    rsthi = sb("rsthi", A_RSTHI, [1, 1024], BF16)
    rstlo = sb("rstlo", A_RSTLO, [1, 1024], BF16)
    scrm = sb("scrm",  A_SCRM,  [1, 1024], F32)
    scrv = sb("scrv",  A_SCRV,  [1, 1024], F32)
    xhtmp = sb("xhtmp", A_XHTMP, [128, 512], F32)
    ident = sb("ident_sb", A_IDENT, [128, 128], BF16)
    ones = sb("ones_sb", A_ONES, [128, 128], BF16)
    svec = sb("svec_sb", A_SVEC, [128, SV_COLS], F32)
    recrow = sb("recrow", A_RECR, [1, 1024], F32)
    rec2h = sb("rec2h", A_REC2H, [1, 1024], BF16)
    rec2l = sb("rec2l", A_REC2L, [1, 1024], BF16)
    Vp = [sb("Vp0", A_VP0, [128, 1040], BF16), sb("Vp1", A_VP1, [128, 1040], BF16)]
    rrep = [sb("rrep0", A_RREP0, [64, 512], F32), sb("rrep1", A_RREP1, [64, 512], F32)]
    czero = sb("czero", A_CZERO, [128, 8], F32)
    silu = sb("silu_sb", A_SILU, [128, 1024], BF16)
    fftmp = sb("fftmp", A_FFTMP, [128, 512], F32)

    nc.const_aps.register(F32, 0.0, czero[:, 0:1])

    pB0 = E(nc.psum_tensor("pB0", [128, 512], F32))
    pB1 = E(nc.psum_tensor("pB1", [128, 512], F32))
    psg = {}
    psh = {}

    def ps_alloc(name, shape, dtype):
        g = nc.psum_tensor(name + str(nc.next_id()), shape, dtype)
        psg[name] = g
        psh[name] = g.__enter__()
        return psh[name]

    def ps_free(*names):
        for n in names:
            psg.pop(n).__exit__(None, None, None)
            psh.pop(n)

    ps_alloc("ptr0", [128, 1024], BF16)
    ps_alloc("ptr1", [128, 1024], BF16)
    cur = {"pA": ps_alloc("pA", [128, 2048], F32)}

    sem = {k: E(nc.semaphore(k)) for k in ("dW", "dG", "V", "S", "P", "G")}
    cnt = {k: 0 for k in sem}
    ops = {k: [] for k in ("t", "v", "s", "q", "g")}
    lastw = {e: {k: 0 for k in sem} for e in ops}

    def emit(e, fn):
        ops[e].append(fn)

    def inc(sname, k=1):
        cnt[sname] += k
        return cnt[sname]

    def wait(e, sname, val):
        if val <= 0 or val <= lastw[e][sname]:
            return
        lastw[e][sname] = val
        emit(e, lambda eng, s=sem[sname], v=val: eng.wait_ge(s, v))

    res = {}
    def acquire(e, r):
        for sname, v in res.get(r, ()):
            wait(e, sname, v)
    def release(r, sname, val):
        res[r] = [(sname, val)]
    def release_multi(r, evs):
        res[r] = list(evs)

    def boot():
        v0 = inc("dG")
        emit("g", lambda g: g.dma_start(svec[:, 0:168], dr["svec"]).then_inc(sem["dG"], 16))
        emit("g", lambda g: g.dma_start(ident[:], dr["ident"]).then_inc(sem["dG"], 16))
        emit("g", lambda g: g.dma_start(maskw[:], dr["maskc"]).then_inc(sem["dG"], 16))
        inc("dG"); inc("dG")
        xv = None
        for c in range(NCH):
            xv = inc("dG")
            emit("g", lambda g, c=c: g.dma_start(
                xT[:, c * 2048:(c + 1) * 2048],
                dr["xT"][c * 128:(c + 1) * 128, :]).then_inc(sem["dG"], 16))
        wait("v", "dG", 16 * 3)
        emit("v", lambda v: v.memset(czero[:], 0.0))
        emit("v", lambda v: v.memset(ones[:], 1.0))
        emit("v", lambda v: v.tensor_scalar(
            svec[:, SV_L0SC2:SV_L0SC2 + 8], svec[:, SV_L0SC:SV_L0SC + 8],
            2.0, None, ALU.mult))
        vv = inc("V")
        emit("v", lambda v: v.nop().then_inc(sem["V"], 1))
        release("xT_ready", "dG", 16 * xv)
        release("misc_ready", "V", vv)

    def emit_ln(src_ap, half, tag):
        wait("s", "V", cnt["V"])
        slots = [(pB0, 0), (pB0, 32), (pB0, 64), (pB1, 0)]
        acquire("t", "pB0"); acquire("t", "pB1")
        last_mm = {}
        for c in range(NCH):
            acquire("v", "xbf")
            wait("v", "dG", res.get("xT_ready", [("dG", 0)])[0][1] if tag == "ln_first" else 0)
            vv = inc("V")
            emit("v", lambda v, c=c: v.tensor_copy(xbf[:], src_ap(c)).then_inc(sem["V"], 1))
            acquire("s", "sqbf")
            if tag == "ln_first":
                wait("s", "dG", res.get("xT_ready", [("dG", 0)])[0][1])
            sv = inc("S")
            emit("s", lambda s, c=c: s.activation(sqbf[:], src_ap(c), AF.Square).then_inc(sem["S"], 1))
            wait("t", "V", vv); wait("t", "S", sv)
            for tb in range(2):
                for kind in range(2):
                    pt, row = slots[tb * 2 + kind]
                    srcb = xbf if kind == 0 else sqbf
                    first, lastc = (c == 0), (c == NCH - 1)
                    last_of_chunk = (tb == 1 and kind == 1)
                    if last_of_chunk:
                        pv = inc("P")
                        if lastc:
                            last_mm[(tb, kind)] = pv
                        emit("t", lambda t, pt=pt, row=row, srcb=srcb, tb=tb, first=first, lastc=lastc:
                             t.matmul(pt[row:row + 1, 0:512],
                                      ones[:, 0:1], srcb[:, tb * 512:(tb + 1) * 512],
                                      start=first, stop=lastc).then_inc(sem["P"], 1))
                        release("xbf", "P", pv)
                        release("sqbf", "P", pv)
                    else:
                        if lastc:
                            pv = inc("P"); last_mm[(tb, kind)] = pv
                            emit("t", lambda t, pt=pt, row=row, srcb=srcb, tb=tb, first=first:
                                 t.matmul(pt[row:row + 1, 0:512],
                                          ones[:, 0:1], srcb[:, tb * 512:(tb + 1) * 512],
                                          start=first, stop=True).then_inc(sem["P"], 1))
                        else:
                            emit("t", lambda t, pt=pt, row=row, srcb=srcb, tb=tb, first=first:
                                 t.matmul(pt[row:row + 1, 0:512],
                                          ones[:, 0:1], srcb[:, tb * 512:(tb + 1) * 512],
                                          start=first, stop=False))
        last_mm[(1, 1)] = cnt["P"]
        for tb in range(2):
            pt_s, row_s = slots[tb * 2]
            pt_q, row_q = slots[tb * 2 + 1]
            wait("v", "P", last_mm[(tb, 0)]); wait("v", "P", last_mm[(tb, 1)])
            o = tb * 512
            emit("v", lambda v, pt_s=pt_s, row_s=row_s, o=o: v.tensor_scalar(
                scrm[0:1, o:o + 512], pt_s[row_s:row_s + 1, 0:512], 1.0 / D, None, ALU.mult))
            emit("v", lambda v, o=o: v.tensor_copy(meanb[0:1, o:o + 512], scrm[0:1, o:o + 512]))
            emit("v", lambda v, o=o: v.tensor_mul(scrv[0:1, o:o + 512], scrm[0:1, o:o + 512], scrm[0:1, o:o + 512]))
            emit("v", lambda v, pt_q=pt_q, row_q=row_q, o=o: v.scalar_tensor_tensor(
                scrv[0:1, o:o + 512], pt_q[row_q:row_q + 1, 0:512], 1.0 / D,
                scrv[0:1, o:o + 512], ALU.mult, ALU.subtract))
            vv = inc("V")
            emit("v", lambda v, o=o: v.tensor_scalar(
                scrv[0:1, o:o + 512], scrv[0:1, o:o + 512], 1e-5, None, ALU.add).then_inc(sem["V"], 1))
            wait("s", "V", vv)
            emit("s", lambda s, o=o: s.activation(scrm[0:1, o:o + 512], scrv[0:1, o:o + 512], AF.Ln))
            sv = inc("S")
            emit("s", lambda s, o=o: s.activation(scrv[0:1, o:o + 512], scrm[0:1, o:o + 512],
                                                  AF.Exp, scale=f32(-0.5)).then_inc(sem["S"], 1))
            wait("v", "S", sv)
            emit("v", lambda v, o=o: v.tensor_copy(rsthi[0:1, o:o + 512], scrv[0:1, o:o + 512]))
            vv = inc("V")
            emit("v", lambda v, o=o: v.tensor_sub(rstlo[0:1, o:o + 512], scrv[0:1, o:o + 512],
                                                  rsthi[0:1, o:o + 512]).then_inc(sem["V"], 1))
            if tb == 1:
                release("pB0", "V", vv); release("pB1", "V", vv)
        wait("t", "V", cnt["V"])
        for tb in range(2):
            o = tb * 512
            acquire("t", "pB0"); acquire("t", "pB1")
            pm = inc("P")
            emit("t", lambda t, o=o: t.matmul(pB0[:, 0:512], ones[0:1, 0:128],
                                              meanb[0:1, o:o + 512], start=True, stop=True).then_inc(sem["P"], 1))
            emit("t", lambda t, o=o: t.matmul(pB1[:, 0:512], ones[0:1, 0:128],
                                              rsthi[0:1, o:o + 512], start=True, stop=False))
            pr = inc("P")
            emit("t", lambda t, o=o: t.matmul(pB1[:, 0:512], ones[0:1, 0:128],
                                              rstlo[0:1, o:o + 512], start=False, stop=True).then_inc(sem["P"], 1))
            wait("v", "P", pr)
            vv = None
            for c in range(NCH):
                emit("v", lambda v, c=c, o=o: v.tensor_sub(
                    xhtmp[:], src_ap(c)[:, o:o + 512], pB0[:, 0:512]))
                vv = inc("V")
                emit("v", lambda v, c=c, o=o: v.tensor_mul(
                    xhat[:, c * 1024 + o: c * 1024 + o + 512], xhtmp[:], pB1[:, 0:512]).then_inc(sem["V"], 1))
            release("pB0", "V", vv); release("pB1", "V", vv)
        release("xhat_ready", "V", cnt["V"])
        release("Dreg", "V", cnt["V"])

    def linear_pass(W, G, IC, NTB, rhs_fn, evac_fn, pAx, cast="copy", scale_groups=(),
                    rhs_ready=None, slot_base=0, nslots=4):
        NP = max(1, (IC * 128) // 1024)
        PI = IC // NP
        for g in range(G):
            b = g % 2
            cast_done = None
            for pi in range(NP):
                wl = (g * NP + pi) % 2
                if g == 0 and pi == 0:
                    acquire("q", "Dreg")
                acquire("q", "wld%d" % wl)
                dv = inc("dW")
                emit("q", lambda q, W=W, g=g, pi=pi, wl=wl, PI=PI:
                     q.dma_start(wld[wl][:, 0:PI * 128].rearrange("p (a c) -> p a c", c=128),
                                 W[g, :, pi * PI:(pi + 1) * PI, :]).then_inc(sem["dW"], 16))
                wait("v", "dW", 16 * dv)
                if pi == 0:
                    acquire("v", "wbf%d" % b)
                dst = lambda: wbf[b][:, pi * PI * 128:(pi + 1) * PI * 128]
                cast_done = inc("V")
                if cast == "copy" or (cast == "scale" and g not in scale_groups):
                    emit("v", lambda v, b=b, pi=pi, PI=PI, wl=wl: v.tensor_copy(
                        wbf[b][:, pi * PI * 128:(pi + 1) * PI * 128],
                        wld[wl][:, 0:PI * 128]).then_inc(sem["V"], 1))
                elif cast == "scale":
                    emit("v", lambda v, b=b, pi=pi, PI=PI, wl=wl: v.tensor_scalar(
                        wbf[b][:, pi * PI * 128:(pi + 1) * PI * 128],
                        wld[wl][:, 0:PI * 128], 0.125, None, ALU.mult).then_inc(sem["V"], 1))
                elif cast == "bin":
                    emit("v", lambda v, b=b, pi=pi, PI=PI, wl=wl: v.tensor_scalar(
                        wbf[b][:, pi * PI * 128:(pi + 1) * PI * 128],
                        wld[wl][:, 0:PI * 128], 0.0, 0.5, ALU.is_ge, ALU.subtract).then_inc(sem["V"], 1))
                release("wld%d" % wl, "V", cast_done)
            wait("t", "V", cast_done)
            if rhs_ready is not None and g == 0:
                wait("t", rhs_ready[0], rhs_ready[1])
            last_of_group = None
            for tb in range(NTB):
                slot = slot_base + (g * NTB + tb) % nslots
                acquire("t", "pA%d" % slot)
                for ic in range(IC):
                    first, last = (ic == 0), (ic == IC - 1)
                    if last:
                        pv = inc("P"); last_of_group = pv
                        emit("t", lambda t, b=b, ic=ic, slot=slot, tb=tb, first=first:
                             t.matmul(pAx[:, slot * 512:(slot + 1) * 512],
                                      wbf[b][:, ic * 128:(ic + 1) * 128],
                                      rhs_fn(ic, tb), start=first, stop=True).then_inc(sem["P"], 1))
                    else:
                        emit("t", lambda t, b=b, ic=ic, slot=slot, tb=tb, first=first:
                             t.matmul(pAx[:, slot * 512:(slot + 1) * 512],
                                      wbf[b][:, ic * 128:(ic + 1) * 128],
                                      rhs_fn(ic, tb), start=first, stop=False))
                evac_fn(g, tb, slot, cnt["P"])
            release("wbf%d" % b, "P", last_of_group)
            if g == G - 1:
                release_multi("Dreg", [("P", last_of_group), ("V", cast_done)])

    def mk_evac_copy(dst_fn, pAx):
        def ev(g, tb, slot, mm_done):
            wait("s", "P", mm_done)
            sv = inc("S")
            emit("s", lambda s, g=g, tb=tb, slot=slot: s.copy(
                dst_fn(g, tb), pAx[:, slot * 512:(slot + 1) * 512]).then_inc(sem["S"], 1))
            release("pA%d" % slot, "S", sv)
        return ev

    def mk_evac_resid(bout_col, pAx):
        def ev(g, tb, slot, mm_done):
            wait("v", "P", mm_done)
            vv = inc("V")
            emit("v", lambda v, g=g, tb=tb, slot=slot: v.scalar_tensor_tensor(
                xT[:, g * 2048 + tb * 512: g * 2048 + (tb + 1) * 512],
                pAx[:, slot * 512:(slot + 1) * 512],
                svec[:, bout_col + g: bout_col + g + 1],
                xT[:, g * 2048 + tb * 512: g * 2048 + (tb + 1) * 512],
                ALU.add, ALU.add).then_inc(sem["V"], 1))
            release("pA%d" % slot, "V", vv)
        return ev

    def emit_attn(li):
        ptr = [psh["ptr0"], psh["ptr1"]]
        pA = cur["pA"]
        slot_rr = [0]
        wait("g", "P", cnt["P"])
        wait("g", "V", cnt["V"])
        for hb in range(2):
            emit("g", lambda g, hb=hb: g.memset(
                Vp[hb][:].rearrange("p (a c) -> p a c", c=65)[:, :, 64:65], 1.0))
        gv = inc("G")
        emit("g", lambda g: g.nop().then_inc(sem["G"], 1))
        release("vp_ones", "G", gv)

        tail_pend = []

        def flush_tail():
            while tail_pend:
                tail_pend.pop(0)()

        for h in range(NH):
            hp = 64 * (h % 2)
            hb = h % 2
            qc = (h // 2) * 2048
            kc = (8 + h // 2) * 2048
            vc = (16 + h // 2) * 2048
            acquire("t", "ptr%d" % hb)
            pv = None
            for kt in range(16):
                last = (kt == 15)
                if last:
                    pv = inc("P")
                    emit("t", lambda t, vc=vc, hp=hp, kt=kt, hb=hb, ptr=ptr: t.transpose(
                        ptr[hb][:, kt * 64:(kt + 1) * 64],
                        qkvT[hp:hp + 64, vc + kt * 128: vc + (kt + 1) * 128],
                        ident[hp:hp + 64, hp:hp + 64]).then_inc(sem["P"], 1))
                else:
                    emit("t", lambda t, vc=vc, hp=hp, kt=kt, hb=hb, ptr=ptr: t.transpose(
                        ptr[hb][:, kt * 64:(kt + 1) * 64],
                        qkvT[hp:hp + 64, vc + kt * 128: vc + (kt + 1) * 128],
                        ident[hp:hp + 64, hp:hp + 64]))
            acquire("s", "Vp%d" % hb)
            acquire("s", "vp_ones")
            if h < 2:
                acquire("s", "Dreg")
            wait("s", "P", pv)
            sv = inc("S")
            emit("s", lambda s, hb=hb, ptr=ptr: s.copy(
                Vp[hb][:].rearrange("p (a c) -> p a c", c=65)[:, :, 0:64],
                ptr[hb][:, 0:1024].rearrange("p (a c) -> p a c", c=64)).then_inc(sem["S"], 1))
            release("ptr%d" % hb, "S", sv)
            vp_ready = sv

            for qcc in range(4):
                p = qcc % 2
                pvb = pB0 if p == 0 else pB1
                nkt = 4 * qcc + 4
                acquire("t", "pB%d" % p)
                pvv = None
                for kt in range(nkt):
                    slot = slot_rr[0] % 4
                    slot_rr[0] += 1
                    acquire("t", "pA%d" % slot)
                    spv = inc("P")
                    emit("t", lambda t, hp=hp, qc=qc, kc=kc, qcc=qcc, kt=kt, slot=slot, pA=pA:
                         t.matmul(pA[0:128, slot * 512: slot * 512 + 512],
                                  qkvT[hp:hp + 64, kc + kt * 128: kc + (kt + 1) * 128],
                                  qkvT[hp:hp + 64, qc + qcc * 512: qc + qcc * 512 + 512],
                                  start=True, stop=True).then_inc(sem["P"], 1))
                    if kt == 2 and tail_pend:
                        flush_tail()
                    j = kt - 4 * qcc
                    mv = None
                    if j >= 0:
                        wait("v", "P", spv)
                        mv = inc("V")
                        emit("v", lambda v, slot=slot, j=j, pA=pA: v.scalar_tensor_tensor(
                            pA[:, slot * 512:(slot + 1) * 512],
                            pA[:, slot * 512:(slot + 1) * 512],
                            1.0, maskw[:, 384 - j * 128: 896 - j * 128],
                            ALU.mult, ALU.add).then_inc(sem["V"], 1))
                    ptb = kt % 3
                    acquire("s", "pt%d" % ptb)
                    wait("s", "P", spv)
                    if mv is not None:
                        wait("s", "V", mv)
                    se = inc("S")
                    emit("s", lambda s, ptb=ptb, slot=slot, pA=pA: s.activation(
                        pt_sb[ptb][:, 0:512], pA[:, slot * 512:(slot + 1) * 512],
                        AF.Exp, scale=f32(1.0)).then_inc(sem["S"], 1))
                    release("pA%d" % slot, "S", se)
                    wait("t", "S", max(se, vp_ready))
                    pvv = inc("P")
                    emit("t", lambda t, kt=kt, hb=hb, ptb=ptb, pvb=pvb, nkt=nkt: t.matmul(
                        pvb[0:65, 0:512], Vp[hb][:, kt * 65: kt * 65 + 65],
                        pt_sb[ptb][:, 0:512],
                        start=(kt == 0), stop=(kt == nkt - 1)).then_inc(sem["P"], 1))
                    release("pt%d" % ptb, "P", pvv)
                if qcc == 3:
                    release("Vp%d" % hb, "P", pvv)
                wait("v", "P", pvv)
                emit("v", lambda v, p=p, pvb=pvb: v.reciprocal(
                    recrow[0:1, p * 512:(p + 1) * 512], pvb[64:65, 0:512]))
                emit("v", lambda v, p=p: v.tensor_copy(
                    rec2h[0:1, p * 512:(p + 1) * 512], recrow[0:1, p * 512:(p + 1) * 512]))
                vvl = inc("V")
                emit("v", lambda v, p=p: v.tensor_sub(
                    rec2l[0:1, p * 512:(p + 1) * 512], recrow[0:1, p * 512:(p + 1) * 512],
                    rec2h[0:1, p * 512:(p + 1) * 512]).then_inc(sem["V"], 1))

                def tail(p=p, pvb=pvb, hp=hp, qc=qc, qcc=qcc, vvl=vvl):
                    slot = slot_rr[0] % 4
                    slot_rr[0] += 1
                    acquire("t", "pA%d" % slot)
                    wait("t", "V", vvl)
                    emit("t", lambda t, p=p, slot=slot, pA=pA: t.matmul(
                        pA[0:64, slot * 512: slot * 512 + 512], ones[0:1, 0:64],
                        rec2h[0:1, p * 512:(p + 1) * 512], start=True, stop=False))
                    pbc = inc("P")
                    emit("t", lambda t, p=p, slot=slot, pA=pA: t.matmul(
                        pA[0:64, slot * 512: slot * 512 + 512], ones[0:1, 0:64],
                        rec2l[0:1, p * 512:(p + 1) * 512],
                        start=False, stop=True).then_inc(sem["P"], 1))
                    acquire("v", "rrep%d" % p)
                    wait("v", "P", pbc)
                    vr = inc("V")
                    emit("v", lambda v, p=p, slot=slot, pA=pA: v.tensor_copy(
                        rrep[p][0:64, :], pA[0:64, slot * 512: slot * 512 + 512]).then_inc(sem["V"], 1))
                    release("pA%d" % slot, "V", vr)
                    vo = inc("V")
                    emit("v", lambda v, hp=hp, qc=qc, qcc=qcc, p=p, pvb=pvb: v.tensor_mul(
                        qkvT[hp:hp + 64, qc + qcc * 512: qc + qcc * 512 + 512],
                        pvb[0:64, 0:512], rrep[p][0:64, :]).then_inc(sem["V"], 1))
                    release("pB%d" % p, "V", vo)
                    release("rrep%d" % p, "V", vo)

                tail_pend.append(tail)
        flush_tail()
        release("attn_done", "V", cnt["V"])
        release_multi("wbf0", [("P", cnt["P"]), ("V", cnt["V"])])
        release_multi("wbf1", [("P", cnt["P"]), ("V", cnt["V"])])

    def w1w2_pass(half, pAx):
        G, IC = 32, 8
        for g in range(G):
            b = g % 2
            cast_done = None
            for wi, W in enumerate((dr["w1"], dr["w2"])):
                wl = (g * 2 + wi) % 2
                if g == 0 and wi == 0:
                    acquire("q", "Dreg")
                acquire("q", "wld%d" % wl)
                dv = inc("dW")
                emit("q", lambda q, W=W, g=g, wl=wl: q.dma_start(
                    wld[wl][:, 0:1024].rearrange("p (a c) -> p a c", c=128),
                    W[g, :, :, :]).then_inc(sem["dW"], 16))
                wait("v", "dW", 16 * dv)
                if wi == 0:
                    acquire("v", "wbf%d" % b)
                cast_done = inc("V")
                emit("v", lambda v, b=b, wi=wi, wl=wl: v.tensor_copy(
                    wbf[b][:, wi * 1024:(wi + 1) * 1024], wld[wl][:, 0:1024]).then_inc(sem["V"], 1))
                release("wld%d" % wl, "V", cast_done)
            wait("t", "V", cast_done)
            last_mm = None
            for tb in range(2):
                s1, s2 = tb, 2 + tb
                acquire("t", "pA%d" % s1); acquire("t", "pA%d" % s2)
                for ic in range(IC):
                    last = ic == IC - 1
                    if last:
                        p1 = inc("P")
                        emit("t", lambda t, b=b, ic=ic, s1=s1, tb=tb:
                             t.matmul(pAx[:, s1 * 512:(s1 + 1) * 512], wbf[b][:, ic * 128:(ic + 1) * 128],
                                      xhat[:, ic * 1024 + tb * 512: ic * 1024 + (tb + 1) * 512],
                                      start=(ic == 0), stop=True).then_inc(sem["P"], 1))
                    else:
                        emit("t", lambda t, b=b, ic=ic, s1=s1, tb=tb:
                             t.matmul(pAx[:, s1 * 512:(s1 + 1) * 512], wbf[b][:, ic * 128:(ic + 1) * 128],
                                      xhat[:, ic * 1024 + tb * 512: ic * 1024 + (tb + 1) * 512],
                                      start=(ic == 0), stop=False))
                for ic in range(IC):
                    last = ic == IC - 1
                    if last:
                        p2 = inc("P"); last_mm = p2
                        emit("t", lambda t, b=b, ic=ic, s2=s2, tb=tb:
                             t.matmul(pAx[:, s2 * 512:(s2 + 1) * 512], wbf[b][:, 1024 + ic * 128: 1024 + (ic + 1) * 128],
                                      xhat[:, ic * 1024 + tb * 512: ic * 1024 + (tb + 1) * 512],
                                      start=(ic == 0), stop=True).then_inc(sem["P"], 1))
                    else:
                        emit("t", lambda t, b=b, ic=ic, s2=s2, tb=tb:
                             t.matmul(pAx[:, s2 * 512:(s2 + 1) * 512], wbf[b][:, 1024 + ic * 128: 1024 + (ic + 1) * 128],
                                      xhat[:, ic * 1024 + tb * 512: ic * 1024 + (tb + 1) * 512],
                                      start=(ic == 0), stop=False))
                wait("s", "P", p1)
                acquire("s", "silu%d" % tb)
                sv = inc("S")
                emit("s", lambda s, g=g, tb=tb, s1=s1: s.activation(
                    silu[:, tb * 512:(tb + 1) * 512], pAx[:, s1 * 512:(s1 + 1) * 512],
                    AF.Silu, scale=svec[:, SV_W1S + g: SV_W1S + g + 1]).then_inc(sem["S"], 1))
                release("pA%d" % s1, "S", sv)
                wait("v", "P", p2); wait("v", "S", sv)
                vv = inc("V")
                emit("v", lambda v, g=g, tb=tb, s2=s2: v.scalar_tensor_tensor(
                    h_sb[:, g * 1024 + tb * 512: g * 1024 + (tb + 1) * 512],
                    pAx[:, s2 * 512:(s2 + 1) * 512],
                    svec[:, SV_W2S + g: SV_W2S + g + 1],
                    silu[:, tb * 512:(tb + 1) * 512], ALU.mult, ALU.mult).then_inc(sem["V"], 1))
                release("pA%d" % s2, "V", vv)
                release("silu%d" % tb, "V", vv)
            release("wbf%d" % b, "P", last_mm)
            if g == G - 1:
                release_multi("Dreg", [("P", last_mm), ("V", cast_done)])

    boot()
    xsrc = lambda c: xT[:, c * 2048:(c + 1) * 2048]

    STOPPED = [False]

    def dump_and_stop(src_fn):
        wait("g", "V", cnt["V"]); wait("g", "S", cnt["S"]); wait("g", "P", cnt["P"])
        for c in range(NCH):
            inc("dG")
            emit("g", lambda g, c=c: g.dma_start(
                outT[c * 128:(c + 1) * 128, :], src_fn(c)).then_inc(sem["dG"], 16))
        STOPPED[0] = True

    def xsrc_half(half):
        return lambda c: xT[:, c * 2048 + half * 1024: c * 2048 + half * 1024 + 1024]

    def qkv_dst(base_chunk, half):
        return lambda g, tb: qkvT[:, (base_chunk + g) * 2048 + half * 1024 + tb * 512:
                                   (base_chunk + g) * 2048 + half * 1024 + (tb + 1) * 512]

    if stop_after == "boot":
        dump_and_stop(xsrc)

    for li in range(2):
        if STOPPED[0]:
            break
        for half in range(2):
            emit_ln(xsrc_half(half), half, "ln_first" if (li == 0 and half == 0) else "ln")
            if stop_after == ("ln%d" % li) and half == 1:
                wait("g", "V", cnt["V"]); wait("g", "S", cnt["S"]); wait("g", "P", cnt["P"])
                for c in range(NCH):
                    inc("dG")
                    emit("g", lambda g, c=c: g.dma_start(
                        outT[c * 128:(c + 1) * 128, 0:1024],
                        xhat[:, c * 1024:(c + 1) * 1024]).then_inc(sem["dG"], 16))
                STOPPED[0] = True
                break
            xr = ("V", cnt["V"])
            rhs = lambda ic, tb: xhat[:, ic * 1024 + tb * 512: ic * 1024 + (tb + 1) * 512]
            if li == 0:
                linear_pass(dr["wq0"], 8, 8, 2, rhs, mk_evac_copy(qkv_dst(0, half), cur["pA"]), cur["pA"],
                            cast="scale", scale_groups=set(range(8)), rhs_ready=xr)
                linear_pass(dr["wk0"], 8, 8, 2, rhs, mk_evac_copy(qkv_dst(8, half), cur["pA"]), cur["pA"], rhs_ready=xr)
                linear_pass(dr["wv0"], 8, 8, 2, rhs, mk_evac_copy(qkv_dst(16, half), cur["pA"]), cur["pA"], rhs_ready=xr)
            else:
                linear_pass(dr["wqkv1"], 24, 8, 2, rhs, mk_evac_copy(qkv_dst(0, half), cur["pA"]), cur["pA"],
                            cast="scale", scale_groups=set(range(8)), rhs_ready=xr)
        if STOPPED[0]:
            break
        if stop_after in ("q%d" % li, "k%d" % li, "v%d" % li):
            base = {"q": 0, "k": 8, "v": 16}[stop_after[0]]
            dump_and_stop(lambda c: qkvT[:, (base + c) * 2048:(base + c + 1) * 2048])
            break
        wait("t", "S", cnt["S"])
        emit_attn(li)
        if stop_after == ("att%d" % li):
            dump_and_stop(lambda c: qkvT[:, c * 2048:(c + 1) * 2048])
            break
        orhs = lambda ic, tb: qkvT[:, ic * 2048 + tb * 512: ic * 2048 + (tb + 1) * 512]
        wname = "wo0" if li == 0 else "wo1"
        bout_col = SV_BOUT0 if li == 0 else SV_BOUT1
        linear_pass(dr[wname], 8, 8, 4, orhs, mk_evac_resid(bout_col, cur["pA"]), cur["pA"],
                    rhs_ready=("V", cnt["V"]))
        if stop_after == ("l%da" % li):
            break

        if li == 0:
            for half in range(2):
                if stop_after == "skipffn0":
                    break
                emit_ln(xsrc_half(half), half, "ln")
                def ev(g, tb, slot, mm_done, half=half, pAx=cur["pA"]):
                    wait("v", "P", mm_done)
                    emit("v", lambda v, g=g, slot=slot, pAx=pAx: v.tensor_scalar(
                        fftmp[:], pAx[:, slot * 512:(slot + 1) * 512],
                        svec[:, SV_L0SC2 + g: SV_L0SC2 + g + 1],
                        svec[:, SV_L0BI + g: SV_L0BI + g + 1], ALU.mult, ALU.add))
                    emit("v", lambda v: v.tensor_scalar(
                        fftmp[:], fftmp[:], 100.0, -100.0, ALU.min, ALU.max))
                    vv = inc("V")
                    emit("v", lambda v, g=g, tb=tb, half=half: v.scalar_tensor_tensor(
                        xT[:, g * 2048 + half * 1024 + tb * 512: g * 2048 + half * 1024 + (tb + 1) * 512],
                        fftmp[:], 1.0,
                        xT[:, g * 2048 + half * 1024 + tb * 512: g * 2048 + half * 1024 + (tb + 1) * 512],
                        ALU.mult, ALU.add).then_inc(sem["V"], 1))
                    release("pA%d" % slot, "V", vv)
                rhs = lambda ic, tb: xhat[:, ic * 1024 + tb * 512: ic * 1024 + (tb + 1) * 512]
                linear_pass(dr["wf0"], 8, 8, 2, rhs, ev, cur["pA"], cast="bin", rhs_ready=("V", cnt["V"]))
            if stop_after == "l0f":
                break
        else:
            for half in range(2):
                if half == 0:
                    emit_ln(xsrc_half(0), 0, "ln")
                    wait("g", "V", cnt["V"])
                    spv = None
                    for c in range(NCH):
                        spv = inc("dG")
                        emit("g", lambda g, c=c: g.dma_start(
                            xsp[c * 128:(c + 1) * 128, :], xT[:, c * 2048:(c + 1) * 2048]).then_inc(sem["dG"], 16))
                    release("spill_done", "dG", 16 * spv)
                else:
                    acquire("g", "xrel")
                    wait("g", "V", cnt["V"])
                    rl = None
                    for c in range(NCH):
                        rl = inc("dG")
                        emit("g", lambda g, c=c: g.dma_start(
                            xrel[:, c * 1024:(c + 1) * 1024],
                            xsp[c * 128:(c + 1) * 128, 1024:2048]).then_inc(sem["dG"], 16))
                    wait("v", "dG", 16 * rl)
                    wait("s", "dG", 16 * rl)
                    emit_ln(lambda c: xrel[:, c * 1024:(c + 1) * 1024], 1, "ln")
                for s_, v_ in res["spill_done"]:
                    wait("v", s_, v_); wait("s", s_, v_)
                w1w2_pass(half, cur["pA"])
                def gelu_evac(dst, scol, pAx=None):
                    pAx = pAx or cur["pA"]
                    def ev(g, tb, slot, mm_done, pAx=pAx):
                        wait("s", "P", mm_done)
                        sv = inc("S")
                        emit("s", lambda s, g=g, tb=tb, slot=slot, pAx=pAx: s.activation(
                            dst(g, tb), pAx[:, slot * 512:(slot + 1) * 512], AF.Gelu,
                            scale=svec[:, scol + g: scol + g + 1]).then_inc(sem["S"], 1))
                        release("pA%d" % slot, "S", sv)
                    return ev
                hrhs = lambda ic, tb: h_sb[:, ic * 1024 + tb * 512: ic * 1024 + (tb + 1) * 512]
                grhs = lambda ic, tb: g_sb[:, ic * 1024 + tb * 512: ic * 1024 + (tb + 1) * 512]
                gdst = lambda g, tb: g_sb[:, g * 1024 + tb * 512: g * 1024 + (tb + 1) * 512]
                fdst = lambda g, tb: h_sb[:, g * 1024 + tb * 512: g * 1024 + (tb + 1) * 512]
                linear_pass(dr["h1"], 32, 32, 2, hrhs, gelu_evac(gdst, SV_H1S),
                            cur["pA"], rhs_ready=("V", cnt["V"]))
                linear_pass(dr["h2"], 32, 32, 2, grhs, gelu_evac(fdst, SV_H2S),
                            cur["pA"], rhs_ready=("S", cnt["S"]))
                if half == 0:
                    acquire("g", "xrel")
                    for s_, v_ in res["spill_done"]:
                        wait("g", s_, v_)
                    wait("g", "P", cnt["P"])
                    rl = None
                    for c in range(NCH):
                        rl = inc("dG")
                        emit("g", lambda g, c=c: g.dma_start(
                            xrel[:, c * 1024:(c + 1) * 1024],
                            xsp[c * 128:(c + 1) * 128, 0:1024]).then_inc(sem["dG"], 16))
                    wait("v", "dG", 16 * rl)
                out_done = {}
                def w3_evac(g, tb, slot, mm_done, half=half, pAx=cur["pA"]):
                    wait("v", "P", mm_done)
                    vv = inc("V")
                    emit("v", lambda v, g=g, tb=tb, slot=slot, pAx=pAx: v.scalar_tensor_tensor(
                        xrel[:, g * 1024 + tb * 512: g * 1024 + (tb + 1) * 512],
                        pAx[:, slot * 512:(slot + 1) * 512],
                        svec[:, SV_W3S + g: SV_W3S + g + 1],
                        xrel[:, g * 1024 + tb * 512: g * 1024 + (tb + 1) * 512],
                        ALU.mult, ALU.add).then_inc(sem["V"], 1))
                    release("pA%d" % slot, "V", vv)
                    out_done[(g, tb)] = vv
                frhs = lambda ic, tb: h_sb[:, ic * 1024 + tb * 512: ic * 1024 + (tb + 1) * 512]
                linear_pass(dr["w3"], 8, 32, 2, frhs, w3_evac, cur["pA"], rhs_ready=("S", cnt["S"]))
                ov = None
                for c in range(NCH):
                    wait("g", "V", out_done[(c, 1)])
                    ov = inc("dG")
                    emit("g", lambda g, c=c, half=half: g.dma_start(
                        outT[c * 128:(c + 1) * 128, half * 1024:(half + 1) * 1024],
                        xrel[:, c * 1024:(c + 1) * 1024]).then_inc(sem["dG"], 16))
                release("xrel", "dG", 16 * ov)

        if stop_after == "l1a" and li == 1:
            break

    if stop_after != "full" and not STOPPED[0]:
        wait("g", "V", cnt["V"])
        wait("g", "S", cnt["S"])
        for c in range(NCH):
            inc("dG")
            emit("g", lambda g, c=c: g.dma_start(
                outT[c * 128:(c + 1) * 128, :], xT[:, c * 2048:(c + 1) * 2048]).then_inc(sem["dG"], 16))

    wait("g", "dG", 16 * cnt["dG"])
    wait("q", "dW", 16 * cnt["dW"])

    with nc.Block() as block:
        @block.sync
        def _(q):
            for fn in ops["q"]:
                fn(q)

        @block.gpsimd
        def _(g):
            for fn in ops["g"]:
                fn(g)

        @block.vector
        def _(v):
            for fn in ops["v"]:
                fn(v)

        @block.scalar
        def _(s):
            for fn in ops["s"]:
                fn(s)

        @block.tensor
        def _(t):
            for fn in ops["t"]:
                fn(t)

    ps_free("pA", "ptr1", "ptr0")
    ctx.close()
    n = sum(len(v) for v in ops.values())
    return nc, n


_NC_CACHE = {}


def _get_nc(stop_after="full"):
    if stop_after not in _NC_CACHE:
        _NC_CACHE[stop_after] = build_nc(stop_after)[0]
    return _NC_CACHE[stop_after]


def _prep_inputs(inputs):
    kk = np.arange(128)[:, None]
    cc = np.arange(896)[None, :]
    mask = np.where(cc < 384 + kk, NEG, 0.0).astype(ml_dtypes.bfloat16)
    svec = np.zeros((128, SV_COLS - 8), dtype=np.float32)
    svec[:, SV_L0SC:SV_L0SC + 8] = _svpack(inputs["L0_ffn_scale"])
    svec[:, SV_L0BI:SV_L0BI + 8] = _svpack(inputs["L0_ffn_bias"])
    svec[:, SV_BOUT1:SV_BOUT1 + 8] = _svpack(inputs["L1_attn_bout"])
    svec[:, SV_W1S:SV_W1S + 32] = _svpack(inputs["L1_ffn_w1_s"])
    svec[:, SV_W2S:SV_W2S + 32] = _svpack(inputs["L1_ffn_w2_s"])
    svec[:, SV_H1S:SV_H1S + 32] = _svpack(inputs["L1_ffn_h1_s"])
    svec[:, SV_H2S:SV_H2S + 32] = _svpack(inputs["L1_ffn_h2_s"])
    svec[:, SV_W3S:SV_W3S + 8] = _svpack(inputs["L1_ffn_w3_s"])
    common = {
        "wq0": _blk(inputs["L0_attn_W_Q"]), "wk0": _blk(inputs["L0_attn_W_K"]),
        "wv0": _blk(inputs["L0_attn_W_V"]), "wo0": _blk(inputs["L0_attn_W_O"]),
        "wf0": _blk(inputs["L0_ffn_weights"]),
        "wqkv1": _blk(inputs["L1_attn_Wqkv"]), "wo1": _blk(inputs["L1_attn_Wout"]),
        "w1": _blk(inputs["L1_ffn_w1_w"]), "w2": _blk(inputs["L1_ffn_w2_w"]),
        "h1": _blk(inputs["L1_ffn_h1_w"]), "h2": _blk(inputs["L1_ffn_h2_w"]),
        "w3": _blk(inputs["L1_ffn_w3_w"]),
        "svec": np.ascontiguousarray(svec),
        "ident": np.eye(128, dtype=ml_dtypes.bfloat16),
        "maskc": mask,
    }
    x = np.asarray(inputs["L0_attn_x"])
    in_maps = []
    for b in range(8):
        m = dict(common)
        m["xT"] = np.ascontiguousarray(x[b].T.astype(np.float32))
        in_maps.append(m)
    return in_maps


def kernel(**inputs):
    in_maps = _prep_inputs(inputs)
    nc = _get_nc("full")
    res = run_bass_kernel_spmd(nc, in_maps, core_ids=list(range(8)))
    out = np.stack([np.asarray(r["outT"]).T for r in res.results], axis=0)
    return np.ascontiguousarray(out.astype(np.float32))


# version 05 (final)
# speedup vs baseline: 1.2911x; 1.0109x over previous
import numpy as np
import ml_dtypes
import concourse.bass as bass
import concourse.mybir as mybir
from concourse.bass_utils import run_bass_kernel_spmd

F32 = mybir.dt.float32
BF16 = mybir.dt.bfloat16
AF = mybir.ActivationFunctionType
ALU = mybir.AluOpType
AX = mybir.AxisListType
f32 = np.float32

T = 2048
HT = 1024
D = 1024
NCH = 8
NH = 16
DH = 64
DHID = 4096
NEG = -1.0e30

A_XT    = 0
A_QKVT  = 65536
A_G     = 65536
A_XREL  = 131072
A_XHAT  = 163840
A_PT0   = 163840
A_PT1   = 164864
A_PT2   = 165888
A_WLD0  = 180224
A_WLD1  = 184320
A_WBF0  = 188416
A_WBF1  = 196608
A_XBF   = 180224
A_SQBF  = 182272
A_MEANB = 184320
A_RSTHI = 186368
A_RSTLO = 188416
A_SCRM  = 190464
A_SCRV  = 194560
A_XHTMP = 198656
A_IDENT = 204800
A_ONES  = 205568
A_SVEC  = 205824
A_CZERO = 206592
A_SILU  = 206656
A_FFTMP = 208704
A_RECR  = 206656
A_REC2H = 198656
A_REC2L = 200704
A_VP0   = 188416
A_VP1   = 190528
A_RREP0 = 192640
A_RREP1 = 194688
A_MSKW  = 210752
ARENA_BYTES = 212800

SV_L0SC = 0
SV_L0BI = 8
SV_BOUT0 = 16
SV_BOUT1 = 24
SV_W1S = 32
SV_W2S = 64
SV_H1S = 96
SV_H2S = 128
SV_W3S = 160
SV_L0SC2 = 168
SV_COLS = 176


def _blk(w):
    O, I = w.shape
    G, IC = O // 128, I // 128
    wt = np.ascontiguousarray(np.asarray(w).T)
    return np.ascontiguousarray(
        wt.reshape(IC, 128, G, 128).transpose(2, 1, 0, 3))


def _svpack(v):
    v = np.asarray(v, dtype=np.float32)
    return np.ascontiguousarray(v.reshape(-1, 128).T)


def build_nc(stop_after="full"):
    nc = bass.Bass("TRN2", target_bir_lowering=False)

    dr = {}
    def din(name, shape, dt=F32):
        dr[name] = nc.dram_tensor(name, shape, dt, kind="ExternalInput").ap()
        return dr[name]

    din("xT", [D, T])
    din("wq0", [8, 128, 8, 128]); din("wk0", [8, 128, 8, 128])
    din("wv0", [8, 128, 8, 128]); din("wo0", [8, 128, 8, 128])
    din("wf0", [8, 128, 8, 128])
    din("wqkv1", [24, 128, 8, 128]); din("wo1", [8, 128, 8, 128])
    din("w1", [32, 128, 8, 128]); din("w2", [32, 128, 8, 128])
    din("h1", [32, 128, 32, 128]); din("h2", [32, 128, 32, 128])
    din("w3", [8, 128, 32, 128])
    din("svec", [128, 168])
    din("ident", [128, 128], BF16)
    din("maskc", [128, 896], BF16)
    outT = nc.dram_tensor("outT", [D, T], F32, kind="ExternalOutput").ap()
    xsp = nc.dram_tensor("xspill", [D, T], F32, kind="Internal").ap()

    from contextlib import ExitStack
    ctx = ExitStack()
    E = ctx.enter_context

    backing = E(nc.sbuf_tensor("arena", [128, ARENA_BYTES], mybir.dt.uint8))
    base = nc.lookup_mloc(backing).addr

    def sb(name, off, shape, dtype):
        h = nc.virtual_sbuf_tensor(name, shape, dtype)
        nc._tensor(h.name, shape, dtype, type="SB")
        ml = nc.lookup_mloc(h)
        ml.allocated = True
        ml.addr = base + off
        return h

    xT   = sb("xT_sb",  A_XT,   [128, 16384], F32)
    h_sb = sb("h_sb",   A_XT,   [128, 32768], BF16)
    qkvT = sb("qkvT",   A_QKVT, [128, 49152], BF16)
    g_sb = sb("g_sb",   A_G,    [128, 32768], BF16)
    xrel = sb("xrel",   A_XREL, [128, 8192],  F32)
    xhat = sb("xhat",   A_XHAT, [128, 8192],  BF16)
    pt_sb = [sb("pt%d" % i, A_PT0 + i * 1024, [128, 512], BF16) for i in range(3)]
    maskw = sb("maskw", A_MSKW, [128, 896], BF16)
    wld  = [sb("wld0", A_WLD0, [128, 1024], F32), sb("wld1", A_WLD1, [128, 1024], F32)]
    wbf  = [sb("wbf0", A_WBF0, [128, 4096], BF16), sb("wbf1", A_WBF1, [128, 4096], BF16)]
    xbf  = sb("xbf",   A_XBF,   [128, 1024], BF16)
    sqbf = sb("sqbf",  A_SQBF,  [128, 1024], BF16)
    meanb = sb("meanb", A_MEANB, [1, 1024], BF16)
    rsthi = sb("rsthi", A_RSTHI, [1, 1024], BF16)
    rstlo = sb("rstlo", A_RSTLO, [1, 1024], BF16)
    scrm = sb("scrm",  A_SCRM,  [1, 1024], F32)
    scrv = sb("scrv",  A_SCRV,  [1, 1024], F32)
    xhtmp = sb("xhtmp", A_XHTMP, [128, 512], F32)
    ident = sb("ident_sb", A_IDENT, [128, 128], BF16)
    ones = sb("ones_sb", A_ONES, [128, 128], BF16)
    svec = sb("svec_sb", A_SVEC, [128, SV_COLS], F32)
    recrow = sb("recrow", A_RECR, [1, 1024], F32)
    rec2h = sb("rec2h", A_REC2H, [1, 1024], BF16)
    rec2l = sb("rec2l", A_REC2L, [1, 1024], BF16)
    Vp = [sb("Vp0", A_VP0, [128, 1040], BF16), sb("Vp1", A_VP1, [128, 1040], BF16)]
    rrep = [sb("rrep0", A_RREP0, [64, 512], F32), sb("rrep1", A_RREP1, [64, 512], F32)]
    czero = sb("czero", A_CZERO, [128, 8], F32)
    silu = sb("silu_sb", A_SILU, [128, 1024], BF16)
    fftmp = sb("fftmp", A_FFTMP, [128, 512], F32)

    nc.const_aps.register(F32, 0.0, czero[:, 0:1])

    pB0 = E(nc.psum_tensor("pB0", [128, 512], F32))
    pB1 = E(nc.psum_tensor("pB1", [128, 512], F32))
    psg = {}
    psh = {}

    def ps_alloc(name, shape, dtype):
        g = nc.psum_tensor(name + str(nc.next_id()), shape, dtype)
        psg[name] = g
        psh[name] = g.__enter__()
        return psh[name]

    def ps_free(*names):
        for n in names:
            psg.pop(n).__exit__(None, None, None)
            psh.pop(n)

    ps_alloc("ptr0", [128, 1024], BF16)
    ps_alloc("ptr1", [128, 1024], BF16)
    cur = {"pA": ps_alloc("pA", [128, 2048], F32)}

    sem = {k: E(nc.semaphore(k)) for k in ("dW", "dG", "V", "S", "P", "G")}
    cnt = {k: 0 for k in sem}
    ops = {k: [] for k in ("t", "v", "s", "q", "g")}
    lastw = {e: {k: 0 for k in sem} for e in ops}

    def emit(e, fn):
        ops[e].append(fn)

    def inc(sname, k=1):
        cnt[sname] += k
        return cnt[sname]

    def wait(e, sname, val):
        if val <= 0 or val <= lastw[e][sname]:
            return
        lastw[e][sname] = val
        emit(e, lambda eng, s=sem[sname], v=val: eng.wait_ge(s, v))

    res = {}
    def acquire(e, r):
        for sname, v in res.get(r, ()):
            wait(e, sname, v)
    def release(r, sname, val):
        res[r] = [(sname, val)]
    def release_multi(r, evs):
        res[r] = list(evs)

    def boot():
        v0 = inc("dG")
        emit("g", lambda g: g.dma_start(svec[:, 0:168], dr["svec"]).then_inc(sem["dG"], 16))
        emit("g", lambda g: g.dma_start(ident[:], dr["ident"]).then_inc(sem["dG"], 16))
        emit("g", lambda g: g.dma_start(maskw[:], dr["maskc"]).then_inc(sem["dG"], 16))
        inc("dG"); inc("dG")
        xv = None
        for c in range(NCH):
            xv = inc("dG")
            emit("g", lambda g, c=c: g.dma_start(
                xT[:, c * 2048:(c + 1) * 2048],
                dr["xT"][c * 128:(c + 1) * 128, :]).then_inc(sem["dG"], 16))
        wait("v", "dG", 16 * 3)
        emit("v", lambda v: v.memset(czero[:], 0.0))
        emit("v", lambda v: v.memset(ones[:], 1.0))
        emit("v", lambda v: v.tensor_scalar(
            svec[:, SV_L0SC2:SV_L0SC2 + 8], svec[:, SV_L0SC:SV_L0SC + 8],
            2.0, None, ALU.mult))
        vv = inc("V")
        emit("v", lambda v: v.nop().then_inc(sem["V"], 1))
        release("xT_ready", "dG", 16 * xv)
        release("misc_ready", "V", vv)

    def emit_ln(src_ap, half, tag):
        wait("s", "V", cnt["V"])
        slots = [(pB0, 0), (pB0, 32), (pB0, 64), (pB1, 0)]
        acquire("t", "pB0"); acquire("t", "pB1")
        last_mm = {}
        for c in range(NCH):
            acquire("v", "xbf")
            wait("v", "dG", res.get("xT_ready", [("dG", 0)])[0][1] if tag == "ln_first" else 0)
            vv = inc("V")
            emit("v", lambda v, c=c: v.tensor_copy(xbf[:], src_ap(c)).then_inc(sem["V"], 1))
            acquire("s", "sqbf")
            if tag == "ln_first":
                wait("s", "dG", res.get("xT_ready", [("dG", 0)])[0][1])
            sv = inc("S")
            emit("s", lambda s, c=c: s.activation(sqbf[:], src_ap(c), AF.Square).then_inc(sem["S"], 1))
            wait("t", "V", vv); wait("t", "S", sv)
            for tb in range(2):
                for kind in range(2):
                    pt, row = slots[tb * 2 + kind]
                    srcb = xbf if kind == 0 else sqbf
                    first, lastc = (c == 0), (c == NCH - 1)
                    last_of_chunk = (tb == 1 and kind == 1)
                    if last_of_chunk:
                        pv = inc("P")
                        if lastc:
                            last_mm[(tb, kind)] = pv
                        emit("t", lambda t, pt=pt, row=row, srcb=srcb, tb=tb, first=first, lastc=lastc:
                             t.matmul(pt[row:row + 1, 0:512],
                                      ones[:, 0:1], srcb[:, tb * 512:(tb + 1) * 512],
                                      start=first, stop=lastc).then_inc(sem["P"], 1))
                        release("xbf", "P", pv)
                        release("sqbf", "P", pv)
                    else:
                        if lastc:
                            pv = inc("P"); last_mm[(tb, kind)] = pv
                            emit("t", lambda t, pt=pt, row=row, srcb=srcb, tb=tb, first=first:
                                 t.matmul(pt[row:row + 1, 0:512],
                                          ones[:, 0:1], srcb[:, tb * 512:(tb + 1) * 512],
                                          start=first, stop=True).then_inc(sem["P"], 1))
                        else:
                            emit("t", lambda t, pt=pt, row=row, srcb=srcb, tb=tb, first=first:
                                 t.matmul(pt[row:row + 1, 0:512],
                                          ones[:, 0:1], srcb[:, tb * 512:(tb + 1) * 512],
                                          start=first, stop=False))
        last_mm[(1, 1)] = cnt["P"]
        for tb in range(2):
            pt_s, row_s = slots[tb * 2]
            pt_q, row_q = slots[tb * 2 + 1]
            wait("v", "P", last_mm[(tb, 0)]); wait("v", "P", last_mm[(tb, 1)])
            o = tb * 512
            emit("v", lambda v, pt_s=pt_s, row_s=row_s, o=o: v.tensor_scalar(
                scrm[0:1, o:o + 512], pt_s[row_s:row_s + 1, 0:512], 1.0 / D, None, ALU.mult))
            emit("v", lambda v, o=o: v.tensor_copy(meanb[0:1, o:o + 512], scrm[0:1, o:o + 512]))
            emit("v", lambda v, o=o: v.tensor_mul(scrv[0:1, o:o + 512], scrm[0:1, o:o + 512], scrm[0:1, o:o + 512]))
            emit("v", lambda v, pt_q=pt_q, row_q=row_q, o=o: v.scalar_tensor_tensor(
                scrv[0:1, o:o + 512], pt_q[row_q:row_q + 1, 0:512], 1.0 / D,
                scrv[0:1, o:o + 512], ALU.mult, ALU.subtract))
            vv = inc("V")
            emit("v", lambda v, o=o: v.tensor_scalar(
                scrv[0:1, o:o + 512], scrv[0:1, o:o + 512], 1e-5, None, ALU.add).then_inc(sem["V"], 1))
            wait("s", "V", vv)
            emit("s", lambda s, o=o: s.activation(scrm[0:1, o:o + 512], scrv[0:1, o:o + 512], AF.Ln))
            sv = inc("S")
            emit("s", lambda s, o=o: s.activation(scrv[0:1, o:o + 512], scrm[0:1, o:o + 512],
                                                  AF.Exp, scale=f32(-0.5)).then_inc(sem["S"], 1))
            wait("v", "S", sv)
            emit("v", lambda v, o=o: v.tensor_copy(rsthi[0:1, o:o + 512], scrv[0:1, o:o + 512]))
            vv = inc("V")
            emit("v", lambda v, o=o: v.tensor_sub(rstlo[0:1, o:o + 512], scrv[0:1, o:o + 512],
                                                  rsthi[0:1, o:o + 512]).then_inc(sem["V"], 1))
            if tb == 1:
                release("pB0", "V", vv); release("pB1", "V", vv)
        wait("t", "V", cnt["V"])
        for tb in range(2):
            o = tb * 512
            acquire("t", "pB0"); acquire("t", "pB1")
            pm = inc("P")
            emit("t", lambda t, o=o: t.matmul(pB0[:, 0:512], ones[0:1, 0:128],
                                              meanb[0:1, o:o + 512], start=True, stop=True).then_inc(sem["P"], 1))
            emit("t", lambda t, o=o: t.matmul(pB1[:, 0:512], ones[0:1, 0:128],
                                              rsthi[0:1, o:o + 512], start=True, stop=False))
            pr = inc("P")
            emit("t", lambda t, o=o: t.matmul(pB1[:, 0:512], ones[0:1, 0:128],
                                              rstlo[0:1, o:o + 512], start=False, stop=True).then_inc(sem["P"], 1))
            wait("v", "P", pr)
            vv = None
            for c in range(NCH):
                emit("v", lambda v, c=c, o=o: v.tensor_sub(
                    xhtmp[:], src_ap(c)[:, o:o + 512], pB0[:, 0:512]))
                vv = inc("V")
                emit("v", lambda v, c=c, o=o: v.tensor_mul(
                    xhat[:, c * 1024 + o: c * 1024 + o + 512], xhtmp[:], pB1[:, 0:512]).then_inc(sem["V"], 1))
            release("pB0", "V", vv); release("pB1", "V", vv)
        release("xhat_ready", "V", cnt["V"])
        release("Dreg", "V", cnt["V"])

    def linear_pass(W, G, IC, NTB, rhs_fn, evac_fn, pAx, cast="copy", scale_groups=(),
                    rhs_ready=None, slot_base=0, nslots=4):
        NP = max(1, (IC * 128) // 1024)
        PI = IC // NP
        for g in range(G):
            b = g % 2
            cast_done = None
            for pi in range(NP):
                wl = (g * NP + pi) % 2
                if g == 0 and pi == 0:
                    acquire("q", "Dreg")
                acquire("q", "wld%d" % wl)
                dv = inc("dW")
                emit("q", lambda q, W=W, g=g, pi=pi, wl=wl, PI=PI:
                     q.dma_start(wld[wl][:, 0:PI * 128].rearrange("p (a c) -> p a c", c=128),
                                 W[g, :, pi * PI:(pi + 1) * PI, :]).then_inc(sem["dW"], 16))
                wait("v", "dW", 16 * dv)
                if pi == 0:
                    acquire("v", "wbf%d" % b)
                dst = lambda: wbf[b][:, pi * PI * 128:(pi + 1) * PI * 128]
                cast_done = inc("V")
                if cast == "copy" or (cast == "scale" and g not in scale_groups):
                    emit("v", lambda v, b=b, pi=pi, PI=PI, wl=wl: v.tensor_copy(
                        wbf[b][:, pi * PI * 128:(pi + 1) * PI * 128],
                        wld[wl][:, 0:PI * 128]).then_inc(sem["V"], 1))
                elif cast == "scale":
                    emit("v", lambda v, b=b, pi=pi, PI=PI, wl=wl: v.tensor_scalar(
                        wbf[b][:, pi * PI * 128:(pi + 1) * PI * 128],
                        wld[wl][:, 0:PI * 128], 0.125, None, ALU.mult).then_inc(sem["V"], 1))
                elif cast == "bin":
                    emit("v", lambda v, b=b, pi=pi, PI=PI, wl=wl: v.tensor_scalar(
                        wbf[b][:, pi * PI * 128:(pi + 1) * PI * 128],
                        wld[wl][:, 0:PI * 128], 0.0, 0.5, ALU.is_ge, ALU.subtract).then_inc(sem["V"], 1))
                release("wld%d" % wl, "V", cast_done)
            wait("t", "V", cast_done)
            if rhs_ready is not None and g == 0:
                wait("t", rhs_ready[0], rhs_ready[1])
            last_of_group = None
            for tb in range(NTB):
                slot = slot_base + (g * NTB + tb) % nslots
                acquire("t", "pA%d" % slot)
                for ic in range(IC):
                    first, last = (ic == 0), (ic == IC - 1)
                    if last:
                        pv = inc("P"); last_of_group = pv
                        emit("t", lambda t, b=b, ic=ic, slot=slot, tb=tb, first=first:
                             t.matmul(pAx[:, slot * 512:(slot + 1) * 512],
                                      wbf[b][:, ic * 128:(ic + 1) * 128],
                                      rhs_fn(ic, tb), start=first, stop=True).then_inc(sem["P"], 1))
                    else:
                        emit("t", lambda t, b=b, ic=ic, slot=slot, tb=tb, first=first:
                             t.matmul(pAx[:, slot * 512:(slot + 1) * 512],
                                      wbf[b][:, ic * 128:(ic + 1) * 128],
                                      rhs_fn(ic, tb), start=first, stop=False))
                evac_fn(g, tb, slot, cnt["P"])
            release("wbf%d" % b, "P", last_of_group)
            if g == G - 1:
                release_multi("Dreg", [("P", last_of_group), ("V", cast_done)])

    def mk_evac_copy(dst_fn, pAx):
        def ev(g, tb, slot, mm_done):
            wait("s", "P", mm_done)
            sv = inc("S")
            emit("s", lambda s, g=g, tb=tb, slot=slot: s.copy(
                dst_fn(g, tb), pAx[:, slot * 512:(slot + 1) * 512]).then_inc(sem["S"], 1))
            release("pA%d" % slot, "S", sv)
        return ev

    def mk_evac_resid(bout_col, pAx):
        def ev(g, tb, slot, mm_done):
            wait("v", "P", mm_done)
            vv = inc("V")
            emit("v", lambda v, g=g, tb=tb, slot=slot: v.scalar_tensor_tensor(
                xT[:, g * 2048 + tb * 512: g * 2048 + (tb + 1) * 512],
                pAx[:, slot * 512:(slot + 1) * 512],
                svec[:, bout_col + g: bout_col + g + 1],
                xT[:, g * 2048 + tb * 512: g * 2048 + (tb + 1) * 512],
                ALU.add, ALU.add).then_inc(sem["V"], 1))
            release("pA%d" % slot, "V", vv)
        return ev

    def emit_attn(li):
        ptr = [psh["ptr0"], psh["ptr1"]]
        pA = cur["pA"]
        slot_rr = [0]
        wait("g", "P", cnt["P"])
        wait("g", "V", cnt["V"])
        for hb in range(2):
            emit("g", lambda g, hb=hb: g.memset(
                Vp[hb][:].rearrange("p (a c) -> p a c", c=65)[:, :, 64:65], 1.0))
        gv = inc("G")
        emit("g", lambda g: g.nop().then_inc(sem["G"], 1))
        release("vp_ones", "G", gv)

        tail_pend = []

        def flush_tail():
            while tail_pend:
                tail_pend.pop(0)()

        for h in range(NH):
            hp = 64 * (h % 2)
            hb = h % 2
            qc = (h // 2) * 2048
            kc = (8 + h // 2) * 2048
            vc = (16 + h // 2) * 2048
            acquire("t", "ptr%d" % hb)
            pv = None
            for kt in range(16):
                last = (kt == 15)
                if last:
                    pv = inc("P")
                    emit("t", lambda t, vc=vc, hp=hp, kt=kt, hb=hb, ptr=ptr: t.transpose(
                        ptr[hb][:, kt * 64:(kt + 1) * 64],
                        qkvT[hp:hp + 64, vc + kt * 128: vc + (kt + 1) * 128],
                        ident[hp:hp + 64, hp:hp + 64]).then_inc(sem["P"], 1))
                else:
                    emit("t", lambda t, vc=vc, hp=hp, kt=kt, hb=hb, ptr=ptr: t.transpose(
                        ptr[hb][:, kt * 64:(kt + 1) * 64],
                        qkvT[hp:hp + 64, vc + kt * 128: vc + (kt + 1) * 128],
                        ident[hp:hp + 64, hp:hp + 64]))
            acquire("s", "Vp%d" % hb)
            acquire("s", "vp_ones")
            if h < 2:
                acquire("s", "Dreg")
            wait("s", "P", pv)
            sv = inc("S")
            emit("s", lambda s, hb=hb, ptr=ptr: s.copy(
                Vp[hb][:].rearrange("p (a c) -> p a c", c=65)[:, :, 0:64],
                ptr[hb][:, 0:1024].rearrange("p (a c) -> p a c", c=64)).then_inc(sem["S"], 1))
            release("ptr%d" % hb, "S", sv)
            vp_ready = sv

            for qcc in range(4):
                p = qcc % 2
                pvb = pB0 if p == 0 else pB1
                nkt = 4 * qcc + 4
                acquire("t", "pB%d" % p)
                pvv = None
                for kt in range(nkt):
                    slot = slot_rr[0] % 4
                    slot_rr[0] += 1
                    acquire("t", "pA%d" % slot)
                    spv = inc("P")
                    emit("t", lambda t, hp=hp, qc=qc, kc=kc, qcc=qcc, kt=kt, slot=slot, pA=pA:
                         t.matmul(pA[0:128, slot * 512: slot * 512 + 512],
                                  qkvT[hp:hp + 64, kc + kt * 128: kc + (kt + 1) * 128],
                                  qkvT[hp:hp + 64, qc + qcc * 512: qc + qcc * 512 + 512],
                                  start=True, stop=True).then_inc(sem["P"], 1))
                    if kt == 2 and tail_pend:
                        flush_tail()
                    j = kt - 4 * qcc
                    mv = None
                    if j >= 0:
                        wait("v", "P", spv)
                        mv = inc("V")
                        emit("v", lambda v, slot=slot, j=j, pA=pA: v.scalar_tensor_tensor(
                            pA[:, slot * 512: slot * 512 + (j + 1) * 128],
                            pA[:, slot * 512: slot * 512 + (j + 1) * 128],
                            1.0, maskw[:, 384 - j * 128: 512],
                            ALU.mult, ALU.add).then_inc(sem["V"], 1))
                    ptb = kt % 3
                    acquire("s", "pt%d" % ptb)
                    wait("s", "P", spv)
                    if mv is not None:
                        wait("s", "V", mv)
                    se = inc("S")
                    emit("s", lambda s, ptb=ptb, slot=slot, pA=pA: s.activation(
                        pt_sb[ptb][:, 0:512], pA[:, slot * 512:(slot + 1) * 512],
                        AF.Exp, scale=f32(1.0)).then_inc(sem["S"], 1))
                    release("pA%d" % slot, "S", se)
                    wait("t", "S", max(se, vp_ready))
                    pvv = inc("P")
                    emit("t", lambda t, kt=kt, hb=hb, ptb=ptb, pvb=pvb, nkt=nkt: t.matmul(
                        pvb[0:65, 0:512], Vp[hb][:, kt * 65: kt * 65 + 65],
                        pt_sb[ptb][:, 0:512],
                        start=(kt == 0), stop=(kt == nkt - 1)).then_inc(sem["P"], 1))
                    release("pt%d" % ptb, "P", pvv)
                if qcc == 3:
                    release("Vp%d" % hb, "P", pvv)
                wait("v", "P", pvv)
                emit("v", lambda v, p=p, pvb=pvb: v.tensor_copy(
                    rec2h[0:1, p * 512:(p + 1) * 512], pvb[64:65, 0:512]))
                vvl = inc("V")
                emit("v", lambda v, p=p, pvb=pvb: v.tensor_sub(
                    rec2l[0:1, p * 512:(p + 1) * 512], pvb[64:65, 0:512],
                    rec2h[0:1, p * 512:(p + 1) * 512]).then_inc(sem["V"], 1))

                def tail(p=p, pvb=pvb, hp=hp, qc=qc, qcc=qcc, vvl=vvl):
                    slot = slot_rr[0] % 4
                    slot_rr[0] += 1
                    acquire("t", "pA%d" % slot)
                    wait("t", "V", vvl)
                    emit("t", lambda t, p=p, slot=slot, pA=pA: t.matmul(
                        pA[0:64, slot * 512: slot * 512 + 512], ones[0:1, 0:64],
                        rec2h[0:1, p * 512:(p + 1) * 512], start=True, stop=False))
                    pbc = inc("P")
                    emit("t", lambda t, p=p, slot=slot, pA=pA: t.matmul(
                        pA[0:64, slot * 512: slot * 512 + 512], ones[0:1, 0:64],
                        rec2l[0:1, p * 512:(p + 1) * 512],
                        start=False, stop=True).then_inc(sem["P"], 1))
                    acquire("v", "rrep%d" % p)
                    wait("v", "P", pbc)
                    vr = inc("V")
                    emit("v", lambda v, p=p, slot=slot, pA=pA: v.reciprocal(
                        rrep[p][0:64, :], pA[0:64, slot * 512: slot * 512 + 512]).then_inc(sem["V"], 1))
                    release("pA%d" % slot, "V", vr)
                    vo = inc("V")
                    emit("v", lambda v, hp=hp, qc=qc, qcc=qcc, p=p, pvb=pvb: v.tensor_mul(
                        qkvT[hp:hp + 64, qc + qcc * 512: qc + qcc * 512 + 512],
                        pvb[0:64, 0:512], rrep[p][0:64, :]).then_inc(sem["V"], 1))
                    release("pB%d" % p, "V", vo)
                    release("rrep%d" % p, "V", vo)

                tail_pend.append(tail)
        flush_tail()
        release("attn_done", "V", cnt["V"])
        release_multi("wbf0", [("P", cnt["P"]), ("V", cnt["V"])])
        release_multi("wbf1", [("P", cnt["P"]), ("V", cnt["V"])])

    def w1w2_pass(half, pAx):
        G, IC = 32, 8
        for g in range(G):
            b = g % 2
            cast_done = None
            for wi, W in enumerate((dr["w1"], dr["w2"])):
                wl = (g * 2 + wi) % 2
                if g == 0 and wi == 0:
                    acquire("q", "Dreg")
                acquire("q", "wld%d" % wl)
                dv = inc("dW")
                emit("q", lambda q, W=W, g=g, wl=wl: q.dma_start(
                    wld[wl][:, 0:1024].rearrange("p (a c) -> p a c", c=128),
                    W[g, :, :, :]).then_inc(sem["dW"], 16))
                wait("v", "dW", 16 * dv)
                if wi == 0:
                    acquire("v", "wbf%d" % b)
                cast_done = inc("V")
                emit("v", lambda v, b=b, wi=wi, wl=wl: v.tensor_copy(
                    wbf[b][:, wi * 1024:(wi + 1) * 1024], wld[wl][:, 0:1024]).then_inc(sem["V"], 1))
                release("wld%d" % wl, "V", cast_done)
            wait("t", "V", cast_done)
            last_mm = None
            for tb in range(2):
                s1, s2 = tb, 2 + tb
                acquire("t", "pA%d" % s1); acquire("t", "pA%d" % s2)
                for ic in range(IC):
                    last = ic == IC - 1
                    if last:
                        p1 = inc("P")
                        emit("t", lambda t, b=b, ic=ic, s1=s1, tb=tb:
                             t.matmul(pAx[:, s1 * 512:(s1 + 1) * 512], wbf[b][:, ic * 128:(ic + 1) * 128],
                                      xhat[:, ic * 1024 + tb * 512: ic * 1024 + (tb + 1) * 512],
                                      start=(ic == 0), stop=True).then_inc(sem["P"], 1))
                    else:
                        emit("t", lambda t, b=b, ic=ic, s1=s1, tb=tb:
                             t.matmul(pAx[:, s1 * 512:(s1 + 1) * 512], wbf[b][:, ic * 128:(ic + 1) * 128],
                                      xhat[:, ic * 1024 + tb * 512: ic * 1024 + (tb + 1) * 512],
                                      start=(ic == 0), stop=False))
                for ic in range(IC):
                    last = ic == IC - 1
                    if last:
                        p2 = inc("P"); last_mm = p2
                        emit("t", lambda t, b=b, ic=ic, s2=s2, tb=tb:
                             t.matmul(pAx[:, s2 * 512:(s2 + 1) * 512], wbf[b][:, 1024 + ic * 128: 1024 + (ic + 1) * 128],
                                      xhat[:, ic * 1024 + tb * 512: ic * 1024 + (tb + 1) * 512],
                                      start=(ic == 0), stop=True).then_inc(sem["P"], 1))
                    else:
                        emit("t", lambda t, b=b, ic=ic, s2=s2, tb=tb:
                             t.matmul(pAx[:, s2 * 512:(s2 + 1) * 512], wbf[b][:, 1024 + ic * 128: 1024 + (ic + 1) * 128],
                                      xhat[:, ic * 1024 + tb * 512: ic * 1024 + (tb + 1) * 512],
                                      start=(ic == 0), stop=False))
                wait("s", "P", p1)
                acquire("s", "silu%d" % tb)
                sv = inc("S")
                emit("s", lambda s, g=g, tb=tb, s1=s1: s.activation(
                    silu[:, tb * 512:(tb + 1) * 512], pAx[:, s1 * 512:(s1 + 1) * 512],
                    AF.Silu, scale=svec[:, SV_W1S + g: SV_W1S + g + 1]).then_inc(sem["S"], 1))
                release("pA%d" % s1, "S", sv)
                wait("v", "P", p2); wait("v", "S", sv)
                vv = inc("V")
                emit("v", lambda v, g=g, tb=tb, s2=s2: v.scalar_tensor_tensor(
                    h_sb[:, g * 1024 + tb * 512: g * 1024 + (tb + 1) * 512],
                    pAx[:, s2 * 512:(s2 + 1) * 512],
                    svec[:, SV_W2S + g: SV_W2S + g + 1],
                    silu[:, tb * 512:(tb + 1) * 512], ALU.mult, ALU.mult).then_inc(sem["V"], 1))
                release("pA%d" % s2, "V", vv)
                release("silu%d" % tb, "V", vv)
            release("wbf%d" % b, "P", last_mm)
            if g == G - 1:
                release_multi("Dreg", [("P", last_mm), ("V", cast_done)])

    boot()
    xsrc = lambda c: xT[:, c * 2048:(c + 1) * 2048]

    STOPPED = [False]

    def dump_and_stop(src_fn):
        wait("g", "V", cnt["V"]); wait("g", "S", cnt["S"]); wait("g", "P", cnt["P"])
        for c in range(NCH):
            inc("dG")
            emit("g", lambda g, c=c: g.dma_start(
                outT[c * 128:(c + 1) * 128, :], src_fn(c)).then_inc(sem["dG"], 16))
        STOPPED[0] = True

    def xsrc_half(half):
        return lambda c: xT[:, c * 2048 + half * 1024: c * 2048 + half * 1024 + 1024]

    def qkv_dst(base_chunk, half):
        return lambda g, tb: qkvT[:, (base_chunk + g) * 2048 + half * 1024 + tb * 512:
                                   (base_chunk + g) * 2048 + half * 1024 + (tb + 1) * 512]

    if stop_after == "boot":
        dump_and_stop(xsrc)

    for li in range(2):
        if STOPPED[0]:
            break
        for half in range(2):
            emit_ln(xsrc_half(half), half, "ln_first" if (li == 0 and half == 0) else "ln")
            if stop_after == ("ln%d" % li) and half == 1:
                wait("g", "V", cnt["V"]); wait("g", "S", cnt["S"]); wait("g", "P", cnt["P"])
                for c in range(NCH):
                    inc("dG")
                    emit("g", lambda g, c=c: g.dma_start(
                        outT[c * 128:(c + 1) * 128, 0:1024],
                        xhat[:, c * 1024:(c + 1) * 1024]).then_inc(sem["dG"], 16))
                STOPPED[0] = True
                break
            xr = ("V", cnt["V"])
            rhs = lambda ic, tb: xhat[:, ic * 1024 + tb * 512: ic * 1024 + (tb + 1) * 512]
            if li == 0:
                linear_pass(dr["wq0"], 8, 8, 2, rhs, mk_evac_copy(qkv_dst(0, half), cur["pA"]), cur["pA"],
                            cast="scale", scale_groups=set(range(8)), rhs_ready=xr)
                linear_pass(dr["wk0"], 8, 8, 2, rhs, mk_evac_copy(qkv_dst(8, half), cur["pA"]), cur["pA"], rhs_ready=xr)
                linear_pass(dr["wv0"], 8, 8, 2, rhs, mk_evac_copy(qkv_dst(16, half), cur["pA"]), cur["pA"], rhs_ready=xr)
            else:
                linear_pass(dr["wqkv1"], 24, 8, 2, rhs, mk_evac_copy(qkv_dst(0, half), cur["pA"]), cur["pA"],
                            cast="scale", scale_groups=set(range(8)), rhs_ready=xr)
        if STOPPED[0]:
            break
        if stop_after in ("q%d" % li, "k%d" % li, "v%d" % li):
            base = {"q": 0, "k": 8, "v": 16}[stop_after[0]]
            dump_and_stop(lambda c: qkvT[:, (base + c) * 2048:(base + c + 1) * 2048])
            break
        wait("t", "S", cnt["S"])
        emit_attn(li)
        if stop_after == ("att%d" % li):
            dump_and_stop(lambda c: qkvT[:, c * 2048:(c + 1) * 2048])
            break
        orhs = lambda ic, tb: qkvT[:, ic * 2048 + tb * 512: ic * 2048 + (tb + 1) * 512]
        wname = "wo0" if li == 0 else "wo1"
        bout_col = SV_BOUT0 if li == 0 else SV_BOUT1
        linear_pass(dr[wname], 8, 8, 4, orhs, mk_evac_resid(bout_col, cur["pA"]), cur["pA"],
                    rhs_ready=("V", cnt["V"]))
        if stop_after == ("l%da" % li):
            break

        if li == 0:
            for half in range(2):
                if stop_after == "skipffn0":
                    break
                emit_ln(xsrc_half(half), half, "ln")
                def ev(g, tb, slot, mm_done, half=half, pAx=cur["pA"]):
                    wait("v", "P", mm_done)
                    emit("v", lambda v, g=g, slot=slot, pAx=pAx: v.tensor_scalar(
                        fftmp[:], pAx[:, slot * 512:(slot + 1) * 512],
                        svec[:, SV_L0SC2 + g: SV_L0SC2 + g + 1],
                        svec[:, SV_L0BI + g: SV_L0BI + g + 1], ALU.mult, ALU.add))
                    emit("v", lambda v: v.tensor_scalar(
                        fftmp[:], fftmp[:], 100.0, -100.0, ALU.min, ALU.max))
                    vv = inc("V")
                    emit("v", lambda v, g=g, tb=tb, half=half: v.scalar_tensor_tensor(
                        xT[:, g * 2048 + half * 1024 + tb * 512: g * 2048 + half * 1024 + (tb + 1) * 512],
                        fftmp[:], 1.0,
                        xT[:, g * 2048 + half * 1024 + tb * 512: g * 2048 + half * 1024 + (tb + 1) * 512],
                        ALU.mult, ALU.add).then_inc(sem["V"], 1))
                    release("pA%d" % slot, "V", vv)
                rhs = lambda ic, tb: xhat[:, ic * 1024 + tb * 512: ic * 1024 + (tb + 1) * 512]
                linear_pass(dr["wf0"], 8, 8, 2, rhs, ev, cur["pA"], cast="bin", rhs_ready=("V", cnt["V"]))
            if stop_after == "l0f":
                break
        else:
            for half in range(2):
                if half == 0:
                    emit_ln(xsrc_half(0), 0, "ln")
                    wait("g", "V", cnt["V"])
                    spv = None
                    for c in range(NCH):
                        spv = inc("dG")
                        emit("g", lambda g, c=c: g.dma_start(
                            xsp[c * 128:(c + 1) * 128, :], xT[:, c * 2048:(c + 1) * 2048]).then_inc(sem["dG"], 16))
                    release("spill_done", "dG", 16 * spv)
                else:
                    acquire("g", "xrel")
                    wait("g", "V", cnt["V"])
                    rl = None
                    for c in range(NCH):
                        rl = inc("dG")
                        emit("g", lambda g, c=c: g.dma_start(
                            xrel[:, c * 1024:(c + 1) * 1024],
                            xsp[c * 128:(c + 1) * 128, 1024:2048]).then_inc(sem["dG"], 16))
                    wait("v", "dG", 16 * rl)
                    wait("s", "dG", 16 * rl)
                    emit_ln(lambda c: xrel[:, c * 1024:(c + 1) * 1024], 1, "ln")
                for s_, v_ in res["spill_done"]:
                    wait("v", s_, v_); wait("s", s_, v_)
                w1w2_pass(half, cur["pA"])
                def gelu_evac(dst, scol, pAx=None):
                    pAx = pAx or cur["pA"]
                    def ev(g, tb, slot, mm_done, pAx=pAx):
                        wait("s", "P", mm_done)
                        sv = inc("S")
                        emit("s", lambda s, g=g, tb=tb, slot=slot, pAx=pAx: s.activation(
                            dst(g, tb), pAx[:, slot * 512:(slot + 1) * 512], AF.Gelu,
                            scale=svec[:, scol + g: scol + g + 1]).then_inc(sem["S"], 1))
                        release("pA%d" % slot, "S", sv)
                    return ev
                hrhs = lambda ic, tb: h_sb[:, ic * 1024 + tb * 512: ic * 1024 + (tb + 1) * 512]
                grhs = lambda ic, tb: g_sb[:, ic * 1024 + tb * 512: ic * 1024 + (tb + 1) * 512]
                gdst = lambda g, tb: g_sb[:, g * 1024 + tb * 512: g * 1024 + (tb + 1) * 512]
                fdst = lambda g, tb: h_sb[:, g * 1024 + tb * 512: g * 1024 + (tb + 1) * 512]
                linear_pass(dr["h1"], 32, 32, 2, hrhs, gelu_evac(gdst, SV_H1S),
                            cur["pA"], rhs_ready=("V", cnt["V"]))
                linear_pass(dr["h2"], 32, 32, 2, grhs, gelu_evac(fdst, SV_H2S),
                            cur["pA"], rhs_ready=("S", cnt["S"]))
                if half == 0:
                    acquire("g", "xrel")
                    for s_, v_ in res["spill_done"]:
                        wait("g", s_, v_)
                    wait("g", "P", cnt["P"])
                    rl = None
                    for c in range(NCH):
                        rl = inc("dG")
                        emit("g", lambda g, c=c: g.dma_start(
                            xrel[:, c * 1024:(c + 1) * 1024],
                            xsp[c * 128:(c + 1) * 128, 0:1024]).then_inc(sem["dG"], 16))
                    wait("v", "dG", 16 * rl)
                out_done = {}
                def w3_evac(g, tb, slot, mm_done, half=half, pAx=cur["pA"]):
                    wait("v", "P", mm_done)
                    vv = inc("V")
                    emit("v", lambda v, g=g, tb=tb, slot=slot, pAx=pAx: v.scalar_tensor_tensor(
                        xrel[:, g * 1024 + tb * 512: g * 1024 + (tb + 1) * 512],
                        pAx[:, slot * 512:(slot + 1) * 512],
                        svec[:, SV_W3S + g: SV_W3S + g + 1],
                        xrel[:, g * 1024 + tb * 512: g * 1024 + (tb + 1) * 512],
                        ALU.mult, ALU.add).then_inc(sem["V"], 1))
                    release("pA%d" % slot, "V", vv)
                    out_done[(g, tb)] = vv
                frhs = lambda ic, tb: h_sb[:, ic * 1024 + tb * 512: ic * 1024 + (tb + 1) * 512]
                linear_pass(dr["w3"], 8, 32, 2, frhs, w3_evac, cur["pA"], rhs_ready=("S", cnt["S"]))
                ov = None
                for c in range(NCH):
                    wait("g", "V", out_done[(c, 1)])
                    ov = inc("dG")
                    emit("g", lambda g, c=c, half=half: g.dma_start(
                        outT[c * 128:(c + 1) * 128, half * 1024:(half + 1) * 1024],
                        xrel[:, c * 1024:(c + 1) * 1024]).then_inc(sem["dG"], 16))
                release("xrel", "dG", 16 * ov)

        if stop_after == "l1a" and li == 1:
            break

    if stop_after != "full" and not STOPPED[0]:
        wait("g", "V", cnt["V"])
        wait("g", "S", cnt["S"])
        for c in range(NCH):
            inc("dG")
            emit("g", lambda g, c=c: g.dma_start(
                outT[c * 128:(c + 1) * 128, :], xT[:, c * 2048:(c + 1) * 2048]).then_inc(sem["dG"], 16))

    wait("g", "dG", 16 * cnt["dG"])
    wait("q", "dW", 16 * cnt["dW"])

    with nc.Block() as block:
        @block.sync
        def _(q):
            for fn in ops["q"]:
                fn(q)

        @block.gpsimd
        def _(g):
            for fn in ops["g"]:
                fn(g)

        @block.vector
        def _(v):
            for fn in ops["v"]:
                fn(v)

        @block.scalar
        def _(s):
            for fn in ops["s"]:
                fn(s)

        @block.tensor
        def _(t):
            for fn in ops["t"]:
                fn(t)

    ps_free("pA", "ptr1", "ptr0")
    ctx.close()
    n = sum(len(v) for v in ops.values())
    return nc, n


_NC_CACHE = {}


def _get_nc(stop_after="full"):
    if stop_after not in _NC_CACHE:
        _NC_CACHE[stop_after] = build_nc(stop_after)[0]
    return _NC_CACHE[stop_after]


def _prep_inputs(inputs):
    kk = np.arange(128)[:, None]
    cc = np.arange(896)[None, :]
    mask = np.where(cc < 384 + kk, NEG, 0.0).astype(ml_dtypes.bfloat16)
    svec = np.zeros((128, SV_COLS - 8), dtype=np.float32)
    svec[:, SV_L0SC:SV_L0SC + 8] = _svpack(inputs["L0_ffn_scale"])
    svec[:, SV_L0BI:SV_L0BI + 8] = _svpack(inputs["L0_ffn_bias"])
    svec[:, SV_BOUT1:SV_BOUT1 + 8] = _svpack(inputs["L1_attn_bout"])
    svec[:, SV_W1S:SV_W1S + 32] = _svpack(inputs["L1_ffn_w1_s"])
    svec[:, SV_W2S:SV_W2S + 32] = _svpack(inputs["L1_ffn_w2_s"])
    svec[:, SV_H1S:SV_H1S + 32] = _svpack(inputs["L1_ffn_h1_s"])
    svec[:, SV_H2S:SV_H2S + 32] = _svpack(inputs["L1_ffn_h2_s"])
    svec[:, SV_W3S:SV_W3S + 8] = _svpack(inputs["L1_ffn_w3_s"])
    common = {
        "wq0": _blk(inputs["L0_attn_W_Q"]), "wk0": _blk(inputs["L0_attn_W_K"]),
        "wv0": _blk(inputs["L0_attn_W_V"]), "wo0": _blk(inputs["L0_attn_W_O"]),
        "wf0": _blk(inputs["L0_ffn_weights"]),
        "wqkv1": _blk(inputs["L1_attn_Wqkv"]), "wo1": _blk(inputs["L1_attn_Wout"]),
        "w1": _blk(inputs["L1_ffn_w1_w"]), "w2": _blk(inputs["L1_ffn_w2_w"]),
        "h1": _blk(inputs["L1_ffn_h1_w"]), "h2": _blk(inputs["L1_ffn_h2_w"]),
        "w3": _blk(inputs["L1_ffn_w3_w"]),
        "svec": np.ascontiguousarray(svec),
        "ident": np.eye(128, dtype=ml_dtypes.bfloat16),
        "maskc": mask,
    }
    x = np.asarray(inputs["L0_attn_x"])
    in_maps = []
    for b in range(8):
        m = dict(common)
        m["xT"] = np.ascontiguousarray(x[b].T.astype(np.float32))
        in_maps.append(m)
    return in_maps


def kernel(**inputs):
    in_maps = _prep_inputs(inputs)
    nc = _get_nc("full")
    res = run_bass_kernel_spmd(nc, in_maps, core_ids=list(range(8)))
    out = np.stack([np.asarray(r["outT"]).T for r in res.results], axis=0)
    return np.ascontiguousarray(out.astype(np.float32))
